# version 05
# baseline (speedup 1.0000x reference)
import numpy as np
import ml_dtypes
from contextlib import ExitStack
from dataclasses import dataclass

import concourse.bass as bass
import concourse.mybir as mybir
from concourse.bass_utils import run_bass_kernel_spmd

F32 = mybir.dt.float32
BF16 = mybir.dt.bfloat16
AF = mybir.ActivationFunctionType
OP = mybir.AluOpType
NCORES = 8


@dataclass
class Cfg:
    H: int = 4096
    DIN: int = 8192
    L: int = 8192
    B: int = 2
    R: int = 256
    N: int = 16
    K: int = 4
    TPB: int = 512
    TCC: int = 1024
    JH: int = 4
    DP: int = 256
    GRP: int = 2048
    eps_ln: float = 1e-5
    eps_rms: float = 1e-6

    @property
    def NTOK(self): return self.B * self.L
    @property
    def TS_TOK(self): return self.NTOK // NCORES
    @property
    def DSL(self): return self.DIN // NCORES
    @property
    def HSL(self): return self.H // NCORES
    @property
    def E(self): return self.R + 2 * self.N
    @property
    def NCB(self): return self.DSL // 128
    @property
    def NHC(self): return self.H // 128
    @property
    def NPAN(self): return self.NTOK // self.TPB
    @property
    def NCHK(self): return self.NTOK // self.TCC
    @property
    def NGRP(self): return self.NTOK // self.GRP
    @property
    def RCH(self): return self.R // 128
    @property
    def NDC(self): return self.DIN // 128

    def subs(self, w):
        out, o = [], 0
        while o < w:
            out.append((o, min(512, w - o)))
            o += 512
        return out


class P:
    ENG_SEM = {"vector": "v", "scalar": "s", "tensor": "t", "gpsimd": "g"}

    def __init__(self, nc, block, mksem):
        self.nc, self.block = nc, block
        self.mksem = mksem
        self.S = {}
        self.cnt = {}
        self.last_wait = {e: {} for e in ["vector", "scalar", "tensor", "gpsimd", "sync"]}
        self.fence_marks = []
        self.fenced = set()
        for k in ("v", "s", "t", "g", "cc"):
            self.sem(k)

    def sem(self, name):
        if name not in self.S:
            self.S[name] = self.mksem(name)
            self.cnt[name] = 0
        return self.S[name]

    def mark(self, sem):
        return (sem, self.cnt[sem])

    def new_phase(self):
        self.fence_marks = [(k, self.cnt[k]) for k in self.S]
        self.fenced = set()

    def _waits(self, eng, eng_name, waits):
        if eng_name not in self.fenced:
            self.fenced.add(eng_name)
            waits = list(self.fence_marks) + list(waits)
        own = self.ENG_SEM.get(eng_name)
        best = {}
        for wm in waits:
            if wm is None:
                continue
            snm, v = wm
            if v <= 0 or snm == own:
                continue
            if v > best.get(snm, 0):
                best[snm] = v
        lw = self.last_wait[eng_name]
        for snm in sorted(best):
            v = best[snm]
            if v > lw.get(snm, 0):
                eng.wait_ge(self.S[snm], v)
                lw[snm] = v

    def on(self, eng_name, waits, body):
        sem = self.ENG_SEM[eng_name]
        me = self

        @getattr(self.block, eng_name)
        def _(eng):
            me._waits(eng, eng_name, waits)
            last = body(eng)
            assert last is not None
            me.cnt[sem] += 1
            if eng_name in ("vector", "scalar", "gpsimd"):
                eng.drain().then_inc(me.S[sem], 1)
            else:
                last.then_inc(me.S[sem], 1)
        return self.mark(sem)

    def dma(self, semname, waits, transfers, queue="sy", slow=False):
        eng_name = "sync" if queue == "sy" else "gpsimd"
        self.sem(semname)
        me = self

        @getattr(self.block, eng_name)
        def _(eng):
            me._waits(eng, eng_name, waits)
            for (o, i) in transfers:
                me.cnt[semname] += 16
                eng.dma_start(out=o, in_=i,
                              allow_slow_non_contiguous=slow).then_inc(me.S[semname], 16)
        return self.mark(semname)

    def cc(self, waits, kind, op, ins, outs):
        me = self

        @self.block.gpsimd
        def _(eng):
            me._waits(eng, "gpsimd", waits)
            me.cnt["cc"] += 1
            eng.collective_compute(kind, op, replica_groups=[list(range(NCORES))],
                                   ins=ins, outs=outs).then_inc(me.S["cc"], 1)
        return self.mark("cc")


def build_program(cfg: Cfg, debug_outputs=False, phases="ABCD"):
    c = cfg
    assert c.TS_TOK % 128 == 0 and c.DSL % 128 == 0 and c.H % 128 == 0
    assert c.R % 128 == 0 and c.N == 16 and c.K == 4
    assert c.L % c.TCC == 0 and c.L % c.TPB == 0
    assert c.GRP % c.TPB == 0 and c.GRP % c.TCC == 0 and c.NTOK % c.GRP == 0
    assert c.GRP % c.DP == 0 and c.NCB % c.JH == 0 and c.DP % 128 == 0

    nc = bass.Bass("TRN2", target_bir_lowering=False, detect_race_conditions=False)

    def din(name, shape, dt=F32):
        return nc.dram_tensor(name, shape, dt, kind="ExternalInput")

    x_tok = din("x_tok", [c.TS_TOK, c.H])
    x_hs = din("x_hs", [c.NTOK, c.HSL])
    w1x = din("w1x", [c.H, c.DSL])
    w1z = din("w1z", [c.H, c.DSL])
    convw = din("convw", [c.DSL, c.K])
    convb = din("convb", [c.DSL, 1])
    xpwT = din("xpwT", [c.DSL, c.E])
    dtwT = din("dtwT", [c.R, c.DSL])
    dtbias = din("dtbias", [c.DSL, 1])
    alog = din("alog", [c.DSL, c.N])
    dpar = din("dpar", [c.DSL, 1])
    dtln = din("dtln", [c.R, 1])
    wBC = din("wBC", [2 * c.N, 1])
    woutT = din("woutT", [c.DIN, c.HSL])
    iT = din("iT", [c.H, c.HSL])
    eye128 = din("eye128", [128, 128], BF16)
    selBC = din("selBC", [2 * c.N, 2 * c.N * 128])
    ones_col = din("ones_col", [128, 1])
    selB_col = din("selB_col", [2 * c.N, 1])
    selC_col = din("selC_col", [2 * c.N, 1])
    sel_bc_rows = din("sel_bc_rows", [1, 64])
    ones_1x128 = din("ones_1x128", [1, 128])

    out_hs = nc.dram_tensor("out_hs", [c.NTOK, c.HSL], F32, kind="ExternalOutput")

    xnt_self = nc.dram_tensor("xnt_self", [c.H, c.TS_TOK], BF16)
    xnt_all = nc.dram_tensor("xnt_all", [NCORES * c.H, c.TS_TOK], BF16)
    spP = [nc.dram_tensor(f"spP{g}", [c.E, c.GRP], F32) for g in range(c.NGRP)]
    spR = [nc.dram_tensor(f"spR{g}", [c.E, c.GRP], F32) for g in range(c.NGRP)]
    xc_dram = nc.dram_tensor("xc_dram", [c.DSL, c.NTOK], F32)
    gz_dram = nc.dram_tensor("gz_dram", [c.DSL, c.NTOK], F32)
    y_self = [nc.dram_tensor(f"y_self{g}", [c.DSL, c.GRP], BF16) for g in range(c.NGRP)]
    y_all = [nc.dram_tensor(f"y_all{g}", [NCORES * c.DSL, c.GRP], BF16) for g in range(c.NGRP)]
    g_dram = nc.dram_tensor("g_dram", [c.NTOK, c.HSL], F32)

    with ExitStack() as ctx:
        E_ = ctx.enter_context
        cst = E_(nc.sbuf_tensor("consts", [128, 4], F32))
        jnk = E_(nc.sbuf_tensor("jnk", [128, 4], F32))
        block = E_(nc.Block())
        p = P(nc, block, lambda name: E_(nc.semaphore(name)))

        def setconsts(v):
            v.memset(cst[:, 0:1], c.eps_ln)
            v.memset(cst[:, 1:2], c.eps_rms)
            v.memset(cst[:, 3:4], 0.0)
            return v.memset(cst[:, 2:3], 1.0)
        m_cst = p.on("vector", [], setconsts)
        nc.const_aps.register(F32, c.eps_ln, cst[:, 0:1])
        nc.const_aps.register(F32, c.eps_rms, cst[:, 1:2])
        nc.const_aps.register(F32, 1.0, cst[:, 2:3])

        NT_A = c.TS_TOK // 128
        with ExitStack() as actx:
            EA = actx.enter_context
            xt = [EA(nc.sbuf_tensor(f"a_xt{i}", [128, c.H], F32)) for i in range(2)]
            xnb = [EA(nc.sbuf_tensor(f"a_xnb{i}", [128, c.H], BF16)) for i in range(2)]
            sqd = EA(nc.sbuf_tensor("a_sqd", [128, c.H], F32))
            stt = EA(nc.sbuf_tensor("a_stat", [128, 8], F32))
            ident = EA(nc.sbuf_tensor("a_ident", [128, 128], BF16))
            tpe = [EA(nc.sbuf_tensor(f"a_tpe{i}", [128, 128], BF16)) for i in range(2)]
            stt_hist = EA(nc.sbuf_tensor("a_stt_hist", [128, 8 * NT_A], F32))
            pst = [EA(nc.psum_tensor(f"a_pst{i}", [128, 128], BF16)) for i in range(2)]

            m_id = p.dma("a_id", [], [(ident[:], eye128.ap())])
            m_ld, m_xn, m_tp, m_ev, m_st = {}, {}, {}, {}, {}
            for i in range(min(2, NT_A)):
                m_ld[i] = p.dma(f"a_ld{i % 2}", [],
                                [(xt[i % 2][:], x_tok.ap()[i * 128:(i + 1) * 128, :])])

            for i in range(NT_A):
                def stats(v, i=i):
                    v.tensor_tensor(sqd[:], xt[i % 2][:], xt[i % 2][:], OP.mult)
                    v.tensor_reduce(stt[:, 0:1], xt[i % 2][:],
                                    axis=mybir.AxisListType.X, op=OP.add)
                    v.tensor_reduce(stt[:, 1:2], sqd[:], axis=mybir.AxisListType.X, op=OP.add)
                    v.tensor_scalar(stt[:, 2:3], stt[:, 0:1], 1.0 / c.H, 0.0, OP.mult, OP.add)
                    v.tensor_scalar(stt[:, 3:4], stt[:, 1:2], 1.0 / c.H, 0.0, OP.mult, OP.add)
                    v.tensor_tensor(stt[:, 4:5], stt[:, 2:3], stt[:, 2:3], OP.mult)
                    v.tensor_copy(jnk[:, 0:1], cst[:, 3:4])
                    return v.tensor_tensor(stt[:, 5:6], stt[:, 3:4], stt[:, 4:5], OP.subtract)
                m_va = p.on("vector", [m_ld[i]], stats)

                def rstd(sc):
                    sc.activation(stt[:, 6:7], stt[:, 5:6], AF.Ln, bias=c.eps_ln)
                    sc.copy(jnk[:, 1:2], cst[:, 3:4])
                    return sc.activation(stt[:, 7:8], stt[:, 6:7], AF.Exp, scale=-0.5)
                m_rs = p.on("scalar", [m_va, m_cst, m_xn.get(i - 1)], rstd)


                w = [m_rs, m_va, m_tp.get((i - 2, c.NHC - 1))]
                m_xn[i] = p.on("vector", w, lambda v, i=i: (
                    v.tensor_scalar(xnb[i % 2][:], xt[i % 2][:], stt[:, 2:3], stt[:, 7:8],
                                    OP.subtract, OP.mult)))
                if i + 2 < NT_A:
                    m_ld[i + 2] = p.dma(f"a_ld{i % 2}", [m_xn[i]], [(
                        xt[i % 2][:], x_tok.ap()[(i + 2) * 128:(i + 3) * 128, :])])

                for hc in range(c.NHC):
                    k = i * c.NHC + hc
                    m_tp[(i, hc)] = p.on("tensor", [m_xn[i], m_id, m_ev.get(k - 2)],
                                         lambda t, i=i, hc=hc, k=k: (
                        t.transpose(pst[k % 2][:], xnb[i % 2][:, hc * 128:(hc + 1) * 128], ident[:])))
                    m_ev[k] = p.on("scalar", [m_tp[(i, hc)], m_st.get(k - 2)],
                                   lambda sc, k=k: sc.copy(tpe[k % 2][:], pst[k % 2][:]))
                    m_st[k] = p.dma(f"a_st{k % 2}", [m_ev[k]], [(
                        xnt_self.ap()[hc * 128:(hc + 1) * 128, i * 128:(i + 1) * 128],
                        tpe[k % 2][:])])

            m_agx = p.cc([m_st[NT_A * c.NHC - 1]], "AllGather", OP.bypass,
                         [xnt_self.ap()], [xnt_all.ap()])

        if "B" not in phases:
            if debug_outputs:
                dbg_xnt = nc.dram_tensor("dbg_xnt", [c.H, c.TS_TOK], BF16, kind="ExternalOutput")
                dbg_stt = nc.dram_tensor("dbg_stt", [128, 8 * (c.TS_TOK // 128)], F32, kind="ExternalOutput")
                allm = [p.mark(k) for k in list(p.S)]
                p.dma("dbg", allm, [(dbg_xnt.ap(), xnt_self.ap()), (dbg_stt.ap(), stt_hist[:])])

            @block.sync
            def _(sy):
                for snm in list(p.S):
                    if snm not in ("v", "s", "t", "g") and p.cnt[snm] > 0:
                        sy.wait_ge(p.S[snm], p.cnt[snm])
            return nc

        p.new_phase()
        PANPG = c.GRP // c.TPB
        EBL = []
        o = 0
        while o < c.E:
            EBL.append((o, min(128, c.E - o)))
            o += 128
        with ExitStack() as bctx:
            EB = bctx.enter_context
            wsb = EB(nc.sbuf_tensor("b_w", [128, c.NHC * c.DSL], BF16))
            xpw = EB(nc.sbuf_tensor("b_xpw", [128, c.NCB * c.E], BF16))
            xk = [[EB(nc.sbuf_tensor(f"b_xk{bu}_{hc}", [128, c.TPB], BF16))
                   for hc in range(c.NHC)] for bu in range(2)]
            xcv = [EB(nc.sbuf_tensor(f"b_xcv{cb}", [128, c.TPB + 3], F32)) for cb in range(c.NCB)]
            cpre = EB(nc.sbuf_tensor("b_cpre", [128, c.TPB], F32))
            xcf = [EB(nc.sbuf_tensor(f"b_xcf{i}", [128, c.TPB], F32)) for i in range(2)]
            xcb = [EB(nc.sbuf_tensor(f"b_xcb{i}", [128, c.TPB], BF16)) for i in range(2)]
            cwt = EB(nc.sbuf_tensor("b_cw", [128, c.NCB * c.K], F32))
            cbt = EB(nc.sbuf_tensor("b_cb", [128, c.NCB], F32))
            spt = [EB(nc.sbuf_tensor(f"b_sp{i}", [ew, c.TPB], F32)) for i, (eo, ew) in enumerate(EBL)]
            gzt = [EB(nc.sbuf_tensor(f"b_gz{i}", [128, c.TPB], F32)) for i in range(2)]
            psx = [EB(nc.psum_tensor(f"b_psx{i}", [128, c.TPB], F32)) for i in range(4)]
            pse = [EB(nc.psum_tensor(f"b_pse{i}", [ew, c.TPB], F32)) for i, (eo, ew) in enumerate(EBL)]

            def wblk(hc, cb):
                q = hc * c.NCB + cb
                return wsb[:, q * 128:(q + 1) * 128]

            m_wx = p.dma("b_w", [], [(wblk(hc, cb),
                                      w1x.ap()[hc * 128:(hc + 1) * 128, cb * 128:(cb + 1) * 128])
                                     for hc in range(c.NHC) for cb in range(c.NCB)], queue="gd")
            tw = [(xpw[:, cb * c.E:(cb + 1) * c.E], xpwT.ap()[cb * 128:(cb + 1) * 128, :])
                  for cb in range(c.NCB)]
            tw.append((cwt[:].rearrange("q (cb k) -> q cb k", k=c.K),
                       convw.ap().rearrange("(cb q) k -> q cb k", q=128)))
            tw.append((cbt[:].unsqueeze(2), convb.ap().rearrange("(cb q) o -> q cb o", q=128)))
            m_waux = p.dma("b_waux", [], tw, queue="gd", slow=True)

            def rank_of(tok):
                return tok // c.TS_TOK, tok % c.TS_TOK

            m_xk = {}
            m_mm = {}
            m_evac = {}
            m_conv = {}
            m_silu = {}
            m_cast = {}
            m_xpmm = {}
            m_spev = {}
            m_xcst = {}
            m_zev = {}
            m_zst = {}
            m_spst = {}
            m_ar = {}

            XPD = 2
            NGX = c.NPAN * c.NCB

            def emit_xpmm(g2):
                pp2, cb2 = g2 // c.NCB, g2 % c.NCB
                w = [m_cast[g2], m_waux]
                if cb2 == 0:
                    w.append(m_spev.get(pp2 - 1))

                def xpmm(t, cb=cb2, gidx=g2):
                    for bi, (eo, ew) in enumerate(EBL):
                        mm = t.matmul(pse[bi][0:ew, :], xpw[:, cb * c.E + eo:cb * c.E + eo + ew],
                                      xcb[gidx % 2][:], start=(cb == 0), stop=(cb == c.NCB - 1))
                    return mm
                m_xpmm[(pp2, cb2)] = p.on("tensor", w, xpmm)
                if cb2 == c.NCB - 1:
                    def spevac(sc, pp=pp2):
                        for bi, (eo, ew) in enumerate(EBL):
                            e = sc.copy(spt[bi][0:ew, :], pse[bi][0:ew, :])
                        return e
                    m_spev[pp2] = p.on("scalar", [m_xpmm[(pp2, c.NCB - 1)], m_spst.get(pp2 - 1)],
                                       spevac)
                    grp2, gl2 = (pp2 * c.TPB) // c.GRP, (pp2 * c.TPB) % c.GRP
                    m_spst[pp2] = p.dma("b_sps", [m_spev[pp2]], [
                        (spP[grp2].ap()[eo:eo + ew, gl2:gl2 + c.TPB], spt[bi][0:ew, :])
                        for bi, (eo, ew) in enumerate(EBL)])
                    if (pp2 + 1) % PANPG == 0:
                        m_ar[grp2] = p.cc([m_spst[pp2]], "AllReduce", OP.add,
                                          [spP[grp2].ap()], [spR[grp2].ap()])

            for pp in range(c.NPAN):
                rk, tl = rank_of(pp * c.TPB)
                m_xk[(0, pp)] = p.dma(f"b_xk{pp % 2}", [m_agx, m_mm.get((0, pp - 2, c.NCB - 1))], [
                    (xk[pp % 2][hc][:],
                     xnt_all.ap()[rk * c.H + hc * 128: rk * c.H + (hc + 1) * 128, tl:tl + c.TPB])
                    for hc in range(c.NHC)])

                for cb in range(c.NCB):
                    gidx = pp * c.NCB + cb
                    bank = gidx % 4

                    def mmx(t, cb=cb, pp=pp, bank=bank):
                        for hc in range(c.NHC):
                            mm = t.matmul(psx[bank][:], wblk(hc, cb), xk[pp % 2][hc][:],
                                          start=(hc == 0), stop=(hc == c.NHC - 1))
                        return mm
                    m_mm[(0, pp, cb)] = p.on(
                        "tensor", [m_wx, m_xk[(0, pp)], m_evac.get(gidx - 4)], mmx)

                    if gidx >= XPD:
                        emit_xpmm(gidx - XPD)

                    m_evac[gidx] = p.on(
                        "scalar", [m_mm[(0, pp, cb)], m_conv.get((pp - 1, cb))],
                        lambda sc, cb=cb, bank=bank: sc.copy(xcv[cb][:, 3:3 + c.TPB], psx[bank][:]))

                    seq_start = (pp * c.TPB) % c.L == 0

                    def conv(v, cb=cb, seq_start=seq_start):
                        vv = xcv[cb]
                        if seq_start:
                            v.memset(vv[:, 0:3], 0.0)
                        v.tensor_scalar(cpre[:], vv[:, 0:c.TPB], cwt[:, cb * c.K:cb * c.K + 1],
                                        cbt[:, cb:cb + 1], OP.mult, OP.add)
                        v.tensor_copy(jnk[:, 0:1], cst[:, 3:4])
                        for k in range(1, c.K):
                            v.scalar_tensor_tensor(cpre[:], vv[:, k:k + c.TPB],
                                                   cwt[:, cb * c.K + k:cb * c.K + k + 1],
                                                   cpre[:], OP.mult, OP.add)
                            if k < c.K - 1:
                                v.tensor_copy(jnk[:, 0:1], cst[:, 3:4])
                        return v.tensor_copy(vv[:, 0:3], vv[:, c.TPB:c.TPB + 3])
                    m_conv[(pp, cb)] = p.on("vector", [m_evac[gidx], m_waux], conv)

                    m_silu[gidx] = p.on(
                        "scalar", [m_conv[(pp, cb)], m_cast.get(gidx - 2), m_xcst.get(gidx - 2)],
                        lambda sc, gidx=gidx: sc.activation(xcf[gidx % 2][:], cpre[:], AF.Silu))

                    m_cast[gidx] = p.on(
                        "gpsimd", [m_silu[gidx], m_xpmm.get(((gidx - 2) // c.NCB, (gidx - 2) % c.NCB))],
                        lambda g, gidx=gidx: g.tensor_copy(xcb[gidx % 2][:], xcf[gidx % 2][:]))

                    m_xcst[gidx] = p.dma(f"b_xcs{gidx % 2}", [m_silu[gidx]], [(
                        xc_dram.ap()[cb * 128:(cb + 1) * 128, pp * c.TPB:(pp + 1) * c.TPB],
                        xcf[gidx % 2][:])])

            for g2 in range(NGX - XPD, NGX):
                emit_xpmm(g2)

            m_wz = p.dma("b_w", [m_mm[(0, c.NPAN - 1, c.NCB - 1)]],
                         [(wblk(hc, cb),
                           w1z.ap()[hc * 128:(hc + 1) * 128, cb * 128:(cb + 1) * 128])
                          for hc in range(c.NHC) for cb in range(c.NCB)], queue="gd")

            for pp in range(c.NPAN):
                rk, tl = rank_of(pp * c.TPB)
                wxk = [m_mm[(1, pp - 2, c.NCB - 1)]] if pp >= 2 else [m_mm[(0, c.NPAN - 1, c.NCB - 1)]]
                m_xk[(1, pp)] = p.dma(f"b_xk{pp % 2}", wxk,
                                      [(xk[pp % 2][hc][:],
                                        xnt_all.ap()[rk * c.H + hc * 128: rk * c.H + (hc + 1) * 128,
                                                     tl:tl + c.TPB]) for hc in range(c.NHC)])

                for cb in range(c.NCB):
                    zi = pp * c.NCB + cb
                    bank = zi % 4

                    def mmz(t, cb=cb, pp=pp, bank=bank):
                        for hc in range(c.NHC):
                            mm = t.matmul(psx[bank][:], wblk(hc, cb), xk[pp % 2][hc][:],
                                          start=(hc == 0), stop=(hc == c.NHC - 1))
                        return mm
                    wz = [m_wz, m_xk[(1, pp)]]
                    if zi >= 4:
                        wz.append(m_zev[zi - 4])
                    else:
                        wz.append(m_evac[c.NPAN * c.NCB - 4 + zi])
                    m_mm[(1, pp, cb)] = p.on("tensor", wz, mmz)

                    m_zev[zi] = p.on("scalar", [m_mm[(1, pp, cb)], m_zst.get(zi - 2)],
                                     lambda sc, bank=bank, zi=zi: (
                        sc.activation(gzt[zi % 2][:], psx[bank][:], AF.Silu)))
                    m_zst[zi] = p.dma(f"b_zs{zi % 2}", [m_zev[zi]], [(
                        gz_dram.ap()[cb * 128:(cb + 1) * 128, pp * c.TPB:(pp + 1) * c.TPB],
                        gzt[zi % 2][:])])

            m_b_done_t = p.mark("t")
            m_b_done_zst = m_zst[c.NPAN * c.NCB - 1]

        p.new_phase()
        NJH = c.NCB // c.JH
        CHKPG = c.GRP // c.TCC
        with ExitStack() as cctx:
            EC = cctx.enter_context
            dtw_bf = EC(nc.sbuf_tensor("c_dtw", [128, c.RCH * c.DSL], BF16))
            selw = EC(nc.sbuf_tensor("c_selw", [2 * c.N, 2 * c.N * 128], F32))
            atile = EC(nc.sbuf_tensor("c_a", [128, c.NCB * c.N], F32))
            dtbt = EC(nc.sbuf_tensor("c_dtb", [128, c.NCB], F32))
            dpt = EC(nc.sbuf_tensor("c_dpt", [128, c.NCB], F32))
            nrm_c = EC(nc.sbuf_tensor("c_nc", [128, 1], F32))
            nrm_s = EC(nc.sbuf_tensor("c_ns", [2 * c.N, 2], F32))
            nrm_r = EC(nc.sbuf_tensor("c_nr", [1, 64 + 128], F32))
            wbct = EC(nc.sbuf_tensor("c_wbc", [2 * c.N, 1], F32))
            dtlnt = EC(nc.sbuf_tensor("c_dtln", [128, c.RCH], F32))
            spq = [EC(nc.sbuf_tensor(f"c_spq{i}", [128, c.TCC], F32)) for i in range(2)]
            spq2 = EC(nc.sbuf_tensor("c_spq2", [2 * c.N, c.TCC], F32))
            scr1 = EC(nc.sbuf_tensor("c_scr1", [128, c.TCC], F32))
            rr = EC(nc.sbuf_tensor("c_rr", [1, 3 * c.TCC], F32))
            dtr_bf = EC(nc.sbuf_tensor("c_dtrbf", [128, c.RCH * c.TCC], BF16))
            bcs = EC(nc.sbuf_tensor("c_bcs", [2 * c.N, c.TCC], F32))
            dt_j = [EC(nc.sbuf_tensor(f"c_dt{j}", [128, c.TCC], F32)) for j in range(c.JH)]
            dtx_j = [EC(nc.sbuf_tensor(f"c_dtx{j}", [128, c.TCC], F32)) for j in range(c.JH)]
            yac_j = [EC(nc.sbuf_tensor(f"c_yac{j}", [128, c.TCC], F32)) for j in range(c.JH)]
            xc_r = [EC(nc.sbuf_tensor(f"c_xcr{i}", [128, c.TCC], F32)) for i in range(c.JH)]
            gz_r = [EC(nc.sbuf_tensor(f"c_gzr{i}", [128, c.TCC], F32)) for i in range(2)]
            cbc_ab = [EC(nc.sbuf_tensor(f"c_cbc{i}", [128, c.TCC], F32)) for i in range(2)]
            da_b = [EC(nc.sbuf_tensor(f"c_da{i}", [128, c.TCC], F32)) for i in range(2)]
            dbx_b = [EC(nc.sbuf_tensor(f"c_dbx{i}", [128, c.TCC], F32)) for i in range(2)]
            hc_b = [EC(nc.sbuf_tensor(f"c_hcb{i}", [128, c.TCC], F32)) for i in range(2)]
            h_all = EC(nc.sbuf_tensor("c_hall", [128, c.JH * c.TCC], F32))
            state = EC(nc.sbuf_tensor("c_state", [128, c.N * c.NCB], F32))
            u_b = EC(nc.sbuf_tensor("c_u", [128, c.TCC], F32))
            yg_b = [EC(nc.sbuf_tensor(f"c_yg{i}", [128, c.TCC], BF16)) for i in range(2)]
            psA = [EC(nc.psum_tensor(f"c_psA{i}", [128, c.TCC], F32)) for i in range(4)]

            tw = [(selw[:], selBC.ap()), (wbct[:], wBC.ap()),
                  *[(dtlnt[:, r:r + 1], dtln.ap()[r * 128:(r + 1) * 128, :]) for r in range(cfg.RCH)],
                  (nrm_c[:, 0:1], ones_col.ap()),
                  (nrm_s[:, 0:1], selB_col.ap()), (nrm_s[:, 1:2], selC_col.ap()),
                  (nrm_r[:, 0:64], sel_bc_rows.ap()), (nrm_r[:, 64:192], ones_1x128.ap()),
                  (atile[:].rearrange("q (cb n) -> q cb n", n=c.N),
                   alog.ap().rearrange("(cb q) n -> q cb n", q=128)),
                  (dtbt[:].unsqueeze(2), dtbias.ap().rearrange("(cb q) o -> q cb o", q=128)),
                  (dpt[:].unsqueeze(2), dpar.ap().rearrange("(cb q) o -> q cb o", q=128))]
            tw += [(spq[r][:, 0:c.DSL], dtwT.ap()[r * 128:(r + 1) * 128, :]) for r in range(c.RCH)]
            m_cld = p.dma("c_cld", [], tw, slow=True)

            m_aexp = p.on("scalar", [m_cld], lambda sc: sc.activation(atile[:], atile[:], AF.Exp))

            def prep_v(v):
                v.tensor_scalar(atile[:], atile[:], -1.0, 0.0, OP.mult, OP.add)
                v.tensor_scalar(selw[:], selw[:], wbct[:, 0:1], 0.0, OP.mult, OP.add)
                for r in range(c.RCH):
                    v.tensor_scalar(dtw_bf[:, r * c.DSL:(r + 1) * c.DSL], spq[r][:, 0:c.DSL],
                                    dtlnt[:, r:r + 1], 0.0, OP.mult, OP.add)
                return v.memset(state[:], 0.0)
            m_prep = p.on("vector", [m_cld, m_aexp], prep_v)

            m_sld = {}
            m_sq = {}
            m_ssum = {}
            m_rst = {}
            m_rbc = {}
            m_scl = {}
            m_dtmm = {}
            m_sftp = {}
            m_dtg = {}
            m_bc = {}
            m_bcev = {}
            m_da = {}
            m_tts = {}
            m_hc = {}
            m_yad = {}
            m_stsv = {}
            m_xcld = {}
            m_gzld = {}
            m_u = {}
            m_yg = {}
            m_ygst = {}
            m_agy = {}
            m_gat_v = {}

            PR = {k: [] for k in ("A0", "A1", "A2", "A3")}

            def scan_prev_idx(ss):
                if ss < 0:
                    return None
                jj2 = ss % c.JH; ss //= c.JH
                n2 = ss % c.N; ss //= c.N
                return (ss // NJH, ss % NJH, n2, jj2)

            for q in range(c.NCHK):
                grp, gl = (q * c.TCC) // c.GRP, (q * c.TCC) % c.GRP
                seq_start = (q * c.TCC) % c.L == 0

                w = [m_ar[grp], m_prep, m_scl.get(q - 1), m_dtg.get((q - 1, c.NCB - 1))]
                m_sld[q] = p.dma("c_sld", w, [
                    (spq[0][:], spR[grp].ap()[0:128, gl:gl + c.TCC]),
                    (spq[1][:], spR[grp].ap()[128:256, gl:gl + c.TCC]),
                    (spq2[:], spR[grp].ap()[256:256 + 2 * c.N, gl:gl + c.TCC])])

                for blk in range(3):
                    src_t = [spq[0], spq[1], spq2][blk]
                    pw = [128, 128, 2 * c.N][blk]
                    w = [m_sld[q], m_ssum.get((q, blk - 1)), m_ssum.get((q - 1, 2))]
                    m_sq[(q, blk)] = p.on("scalar", w, lambda sc, src_t=src_t, pw=pw: (
                        sc.activation(scr1[0:pw, :], src_t[:], AF.Square)))
                    w = [m_sq[(q, blk)]] + PR["A0"]

                    def ssum(t, blk=blk):
                        for (so, sw) in c.subs(c.TCC):
                            if blk == 0:
                                mm = t.matmul(psA[0][0:1, so:so + sw], nrm_c[:, 0:1],
                                              scr1[:, so:so + sw], start=True, stop=False)
                            elif blk == 1:
                                mm = t.matmul(psA[0][0:1, so:so + sw], nrm_c[:, 0:1],
                                              scr1[:, so:so + sw], start=False, stop=True)
                            else:
                                t.matmul(psA[0][32:33, so:so + sw], nrm_s[:, 0:1],
                                         scr1[0:2 * c.N, so:so + sw], start=True, stop=True)
                                mm = t.matmul(psA[0][64:65, so:so + sw], nrm_s[:, 1:2],
                                              scr1[0:2 * c.N, so:so + sw], start=True, stop=True)
                        return mm
                    m_ssum[(q, blk)] = p.on("tensor", w, ssum)
                PR["A0"] = []

                def rstd3(sc, q=q):
                    sc.activation(rr[:, 0:c.TCC], psA[0][0:1, :], AF.Ln,
                                  scale=1.0 / c.R, bias=c.eps_rms)
                    sc.activation(rr[:, c.TCC:2 * c.TCC], psA[0][32:33, :], AF.Ln,
                                  scale=1.0 / c.N, bias=c.eps_rms)
                    sc.activation(rr[:, 0:c.TCC], rr[:, 0:c.TCC], AF.Exp, scale=-0.5)
                    sc.activation(rr[:, 2 * c.TCC:3 * c.TCC], psA[0][64:65, :], AF.Ln,
                                  scale=1.0 / c.N, bias=c.eps_rms)
                    sc.activation(rr[:, c.TCC:2 * c.TCC], rr[:, c.TCC:2 * c.TCC], AF.Exp, scale=-0.5)
                    return sc.activation(rr[:, 2 * c.TCC:3 * c.TCC], rr[:, 2 * c.TCC:3 * c.TCC],
                                         AF.Exp, scale=-0.5)
                m_rst[q] = p.on("scalar", [m_ssum[(q, 0)], m_ssum[(q, 1)], m_ssum[(q, 2)]], rstd3)
                PR["A0"].append(m_rst[q])

                w = [m_rst[q]] + PR["A1"] + PR["A2"]

                def rbc(t, q=q):
                    for (so, sw) in c.subs(c.TCC):
                        t.matmul(psA[1][:, so:so + sw], nrm_r[:, 64:192],
                                 rr[:, so:so + sw], start=True, stop=True)
                        t.matmul(psA[2][0:2 * c.N, so:so + sw], nrm_r[:, 0:32],
                                 rr[:, c.TCC + so:c.TCC + so + sw], start=True, stop=False)
                        mm = t.matmul(psA[2][0:2 * c.N, so:so + sw], nrm_r[:, 32:64],
                                      rr[:, 2 * c.TCC + so:2 * c.TCC + so + sw],
                                      start=False, stop=True)
                    return mm
                m_rbc[q] = p.on("tensor", w, rbc)
                PR["A1"] = []; PR["A2"] = []

                def scl(v, q=q):
                    v.tensor_tensor(dtr_bf[:, 0:c.TCC], spq[0][:], psA[1][:], OP.mult)
                    if c.RCH > 1:
                        v.tensor_tensor(dtr_bf[:, c.TCC:2 * c.TCC], spq[1][:], psA[1][:], OP.mult)
                    return v.tensor_tensor(bcs[:], spq2[:], psA[2][0:2 * c.N, :], OP.mult)
                m_scl[q] = p.on("vector", [m_rbc[q], m_dtmm.get((q - 1, c.NCB - 1))], scl)
                PR["A1"].append(m_scl[q]); PR["A2"].append(m_scl[q])

                for hlf in range(NJH):
                    for jj in range(c.JH):
                        j = hlf * c.JH + jj
                        pdt_name = "A0" if jj % 2 == 0 else "A3"
                        pdt = psA[0] if jj % 2 == 0 else psA[3]
                        w = [m_scl[q], m_rst[q]] + PR[pdt_name]

                        def dtmm(t, jj=jj, j=j, pdt=pdt):
                            for (so, sw) in c.subs(c.TCC):
                                for r in range(c.RCH):
                                    mm = t.matmul(pdt[:, so:so + sw],
                                                  dtw_bf[:, r * c.DSL + j * 128:r * c.DSL + (j + 1) * 128],
                                                  dtr_bf[:, r * c.TCC + so:r * c.TCC + so + sw],
                                                  start=(r == 0), stop=(r == c.RCH - 1))
                            return mm
                        m_dtmm[(q, j)] = p.on("tensor", w, dtmm)
                        PR[pdt_name] = []

                        def sftp(sc, j=j, jj=jj, pdt=pdt):
                            sc.activation(spq[0][:], pdt[:], AF.Abs, bias=dtbt[:, j:j + 1])
                            sc.activation(spq[1][:], pdt[:], AF.Relu, bias=dtbt[:, j:j + 1])
                            sc.activation(spq[0][:], spq[0][:], AF.Exp, scale=-1.0)
                            sc.copy(jnk[:, 1:2], cst[:, 3:4])
                            return sc.activation(spq[0][:], spq[0][:], AF.Ln, bias=1.0)
                        w = [m_dtmm[(q, j)], m_scl[q], m_dtg.get((q, j - 1))]
                        m_sftp[(q, j)] = p.on("scalar", w, sftp)
                        PR[pdt_name].append(m_sftp[(q, j)])

                        pq, pj = (q, (hlf - 1) * c.JH + jj) if hlf >= 1 else (q - 1, (NJH - 1) * c.JH + jj)
                        m_xcld[(q, j)] = p.dma(f"c_xcl{jj}", [m_dtg.get((pq, pj)), m_u.get((pq, pj))],
                                               [(xc_r[jj][:],
                                                 xc_dram.ap()[j * 128:(j + 1) * 128,
                                                              q * c.TCC:(q + 1) * c.TCC])])

                        def dtg(g, j=j, jj=jj):
                            g.tensor_tensor(dt_j[jj][:], spq[0][:], spq[1][:], OP.add)
                            g.tensor_copy(jnk[:, 2:3], cst[:, 3:4])
                            return g.tensor_tensor(dtx_j[jj][:], dt_j[jj][:], xc_r[jj][:], OP.mult)
                        w = [m_sftp[(q, j)], m_xcld[(q, j)]]
                        if (q, hlf) > (0, 0):
                            pqh = (q, hlf - 1) if hlf >= 1 else (q - 1, NJH - 1)
                            w += [m_da.get((pqh[0], pqh[1], c.N - 1, jj)),
                                  m_tts.get((pqh[0], pqh[1], c.N - 1, jj))]
                        m_dtg[(q, j)] = p.on("gpsimd", w, dtg)

                    pend_yad = None
                    for n in range(c.N):
                        m = (q * NJH + hlf) * c.N + n
                        psB_name = "A1" if m % 2 == 0 else "A0"
                        psC_name = "A2" if m % 2 == 0 else "A3"
                        psB = {"A0": psA[0], "A1": psA[1], "A2": psA[2], "A3": psA[3]}[psB_name]
                        psC = {"A0": psA[0], "A1": psA[1], "A2": psA[2], "A3": psA[3]}[psC_name]
                        cbc_t = cbc_ab[m % 2]

                        w = [m_scl[q], m_dtg[(q, hlf * c.JH + c.JH - 1)]] + PR[psB_name] + PR[psC_name]

                        def bc(t, n=n, psB=psB, psC=psC):
                            for (so, sw) in c.subs(c.TCC):
                                t.matmul(psB[:, so:so + sw], selw[:, n * 128:(n + 1) * 128],
                                         bcs[:, so:so + sw], start=True, stop=True)
                                mm = t.matmul(psC[:, so:so + sw],
                                              selw[:, (c.N + n) * 128:(c.N + n + 1) * 128],
                                              bcs[:, so:so + sw], start=True, stop=True)
                            return mm
                        m_bc[(q, hlf, n)] = p.on("tensor", w, bc)
                        PR[psB_name] = []; PR[psC_name] = []

                        w = [m_bc[(q, hlf, n)]]
                        pm2 = scan_prev_idx((m - 2) * c.JH + (c.JH - 1))
                        if pm2 is not None:
                            w.append(m_hc.get(pm2))
                        m_bcev[(q, hlf, n)] = p.on("scalar", w, lambda sc, cbc_t=cbc_t, psC=psC: (
                            sc.copy(cbc_t[:], psC[:])))
                        PR[psC_name].append(m_bcev[(q, hlf, n)])

                        for jj in range(c.JH):
                            j = hlf * c.JH + jj
                            ss = m * c.JH + jj
                            w = [m_dtg[(q, j)]]
                            km2 = scan_prev_idx(ss - 2)
                            if km2 is not None:
                                w.append(m_tts.get(km2))
                            m_da[(q, hlf, n, jj)] = p.on("scalar", w, lambda sc, jj=jj, j=j, n=n, ss=ss: (
                                sc.activation(da_b[ss % 2][:], dt_j[jj][:], AF.Exp,
                                              scale=atile[:, j * c.N + n:j * c.N + n + 1])))

                            w = [m_bc[(q, hlf, n)], m_da[(q, hlf, n, jj)]]
                            w.append(m_hc.get((q, hlf, n - 1, jj)) if n >= 1 else
                                     (m_hc.get((q, hlf - 1, c.N - 1, jj)) if hlf >= 1 else
                                      (m_hc.get((q - 1, NJH - 1, c.N - 1, jj)) if q >= 1 else None)))

                            def scan(v, jj=jj, j=j, n=n, q=q, ss=ss, psB=psB, seq_start=seq_start):
                                v.scalar_tensor_tensor(dbx_b[ss % 2][:], dtx_j[jj][:],
                                                       cst[:, 2:3], psB[:], OP.mult, OP.mult)
                                v.tensor_copy(jnk[:, 0:1], cst[:, 3:4])
                                init = 0.0 if seq_start else state[:, n * c.NCB + j:n * c.NCB + j + 1]
                                return v.tensor_tensor_scan(
                                    h_all[:, jj * c.TCC:(jj + 1) * c.TCC],
                                    da_b[ss % 2][:], dbx_b[ss % 2][:], init, OP.mult, OP.add)
                            m_tts[(q, hlf, n, jj)] = p.on("vector", w, scan)
                            PR[psB_name].append(m_tts[(q, hlf, n, jj)])

                            w = [m_tts[(q, hlf, n, jj)], m_bcev[(q, hlf, n)]]
                            if km2 is not None:
                                w.append(m_yad.get(km2))
                            m_hc[(q, hlf, n, jj)] = p.on("gpsimd", w, lambda g, jj=jj, n=n, ss=ss, cbc_t=cbc_t: (
                                g.tensor_tensor(hc_b[ss % 2][:],
                                                h_all[:, jj * c.TCC:(jj + 1) * c.TCC],
                                                cbc_t[:], OP.mult)))

                            if pend_yad is not None:
                                (pn, pjj, pss) = pend_yad
                                w = [m_hc[(q, hlf, pn, pjj)]]
                                if pn == 0:
                                    m_yad[(q, hlf, pn, pjj)] = p.on("vector", w, lambda v, pjj=pjj, pss=pss: (
                                        v.tensor_copy(yac_j[pjj][:], hc_b[pss % 2][:])))
                                else:
                                    m_yad[(q, hlf, pn, pjj)] = p.on("vector", w, lambda v, pjj=pjj, pss=pss: (
                                        v.scalar_tensor_tensor(yac_j[pjj][:], hc_b[pss % 2][:],
                                                               cst[:, 2:3], yac_j[pjj][:],
                                                               OP.mult, OP.add)))
                            pend_yad = (n, jj, ss)

                        def stsv(v, hlf=hlf, n=n):
                            return v.tensor_copy(
                                state[:, n * c.NCB + hlf * c.JH: n * c.NCB + hlf * c.JH + c.JH],
                                h_all[:].rearrange("z (jj t) -> z jj t", jj=c.JH)[:, :, c.TCC - 1])
                        m_stsv[(q, hlf, n)] = p.on("vector", [m_tts[(q, hlf, n, c.JH - 1)]], stsv)

                    (pn, pjj, pss) = pend_yad
                    w = [m_hc[(q, hlf, pn, pjj)]]
                    if pn == 0:
                        m_yad[(q, hlf, pn, pjj)] = p.on("vector", w, lambda v, pjj=pjj, pss=pss: (
                            v.tensor_copy(yac_j[pjj][:], hc_b[pss % 2][:])))
                    else:
                        m_yad[(q, hlf, pn, pjj)] = p.on("vector", w, lambda v, pjj=pjj, pss=pss: (
                            v.scalar_tensor_tensor(yac_j[pjj][:], hc_b[pss % 2][:],
                                                   cst[:, 2:3], yac_j[pjj][:],
                                                   OP.mult, OP.add)))
                    pend_yad = None

                    for jj in range(c.JH):
                        j = hlf * c.JH + jj
                        w = [m_b_done_zst, m_yg.get((q, hlf, jj - 2))]
                        if jj < 2 and (q, hlf) > (0, 0):
                            pq, ph = (q, hlf - 1) if hlf >= 1 else (q - 1, NJH - 1)
                            w.append(m_yg.get((pq, ph, c.JH + jj - 2)))
                        m_gzld[(q, hlf, jj)] = p.dma(f"c_gzl{jj % 2}", w, [
                            (gz_r[jj % 2][:],
                             gz_dram.ap()[j * 128:(j + 1) * 128, q * c.TCC:(q + 1) * c.TCC])])

                        w = [m_yad[(q, hlf, c.N - 1, jj)]]
                        if jj >= 1:
                            w.append(m_yg.get((q, hlf, jj - 1)))
                        else:
                            pq2, ph2 = (q, hlf - 1) if hlf >= 1 else (q - 1, NJH - 1)
                            w.append(m_yg.get((pq2, ph2, c.JH - 1)))
                        m_u[(q, j)] = p.on("vector", w, lambda v, jj=jj, j=j: (
                            v.scalar_tensor_tensor(u_b[:], xc_r[jj][:], dpt[:, j:j + 1],
                                                   yac_j[jj][:], OP.mult, OP.add)))

                        w = [m_u[(q, j)], m_gzld[(q, hlf, jj)], m_ygst.get((q, hlf, jj - 2))]
                        m_yg[(q, hlf, jj)] = p.on("gpsimd", w, lambda g, jj=jj: (
                            g.tensor_tensor(yg_b[jj % 2][:], u_b[:], gz_r[jj % 2][:], OP.mult)))

                        m_ygst[(q, hlf, jj)] = p.dma(f"c_ygs{jj % 2}", [m_yg[(q, hlf, jj)]], [(
                            y_self[grp].ap()[j * 128:(j + 1) * 128, gl:gl + c.TCC],
                            yg_b[jj % 2][:])])

                if (q + 1) % CHKPG == 0:
                    m_agy[grp] = p.cc([m_ygst[(q, NJH - 1, c.JH - 1)]], "AllGather", OP.bypass,
                                      [y_self[grp].ap()], [y_all[grp].ap()])

        p.new_phase()
        ND = c.NTOK // c.DP
        TCD = c.DP // 128
        with ExitStack() as dctx:
            ED = dctx.enter_context
            itb = ED(nc.sbuf_tensor("d_it", [128, c.NHC * c.HSL], BF16))
            xkd = [[ED(nc.sbuf_tensor(f"d_xk{bu}_{hc}", [128, c.DP], BF16))
                    for hc in range(c.NHC)] for bu in range(2)]
            gtd = [ED(nc.sbuf_tensor(f"d_gt{i}", [128, c.HSL], F32)) for i in range(2)]
            psg = [ED(nc.psum_tensor(f"d_psg{i}", [128, c.HSL], F32)) for i in range(2)]

            m_it = p.dma("d_it", [], [(itb[:, hc * c.HSL:(hc + 1) * c.HSL],
                                       iT.ap()[hc * 128:(hc + 1) * 128, :]) for hc in range(c.NHC)],
                         queue="gd")
            m_xkd, m_gmm, m_gsl, m_gst = {}, {}, {}, {}
            for pp in range(ND):
                rk = (pp * c.DP) // c.TS_TOK
                tl = (pp * c.DP) % c.TS_TOK
                m_xkd[pp] = p.dma(f"d_xkd{pp % 2}", [m_agx, m_gmm.get((pp - 2, TCD - 1))], [
                    (xkd[pp % 2][hc][:],
                     xnt_all.ap()[rk * c.H + hc * 128: rk * c.H + (hc + 1) * 128, tl:tl + c.DP])
                    for hc in range(c.NHC)])
                for tc in range(TCD):
                    k = pp * TCD + tc

                    def gmm(t, pp=pp, tc=tc, k=k):
                        for hc in range(c.NHC):
                            mm = t.matmul(psg[k % 2][:], xkd[pp % 2][hc][:, tc * 128:(tc + 1) * 128],
                                          itb[:, hc * c.HSL:(hc + 1) * c.HSL],
                                          start=(hc == 0), stop=(hc == c.NHC - 1))
                        return mm
                    m_gmm[(pp, tc)] = p.on("tensor", [m_it, m_xkd[pp],
                                                      m_gsl.get((pp, tc - 2) if tc >= 2 else (pp - 1, TCD + tc - 2))], gmm)
                    m_gsl[(pp, tc)] = p.on("scalar", [m_gmm[(pp, tc)],
                                                      m_gst.get((pp, tc - 2) if tc >= 2 else (pp - 1, TCD + tc - 2))],
                                           lambda sc, k=k: sc.activation(gtd[k % 2][:], psg[k % 2][:], AF.Silu))
                    m_gst[(pp, tc)] = p.dma(f"d_gst{(pp * TCD + tc) % 2}", [m_gsl[(pp, tc)]], [(
                        g_dram.ap()[k * 128:(k + 1) * 128, :], gtd[k % 2][:])])
            m_d1_done = m_gst[(ND - 1, TCD - 1)]

        p.new_phase()
        with ExitStack() as dctx2:
            ED = dctx2.enter_context
            wob = ED(nc.sbuf_tensor("d_wo", [128, c.NDC * c.HSL], BF16))
            ypan = [ED(nc.sbuf_tensor(f"d_yp{i}", [128, c.NDC * c.DP], BF16)) for i in range(2)]
            gtl = [ED(nc.sbuf_tensor(f"d_gtl{i}", [128, c.HSL], F32)) for i in range(4)]
            xtl = [ED(nc.sbuf_tensor(f"d_xtl{i}", [128, c.HSL], F32)) for i in range(4)]
            sgt = ED(nc.sbuf_tensor("d_sg", [128, c.HSL], F32))
            otl = [ED(nc.sbuf_tensor(f"d_otl{i}", [128, c.HSL], F32)) for i in range(2)]
            pss = [ED(nc.psum_tensor(f"d_pss{i}", [128, c.HSL], F32)) for i in range(2)]

            m_wo = p.dma("d_wo", [], [(wob[:, dc * c.HSL:(dc + 1) * c.HSL],
                                       woutT.ap()[dc * 128:(dc + 1) * 128, :]) for dc in range(c.NDC)],
                         queue="gd")
            m_yld, m_sld2, m_smm, m_cmb, m_ost = {}, {}, {}, {}, {}
            NYSP = max(1, c.NDC // 8)
            for pp in range(ND):
                grp = (pp * c.DP) // c.GRP
                gl = (pp * c.DP) % c.GRP
                yv = y_all[grp].ap().rearrange("(dc q) t -> q dc t", q=128)
                yp = ypan[pp % 2][:].rearrange("q (dc t) -> q dc t", t=c.DP)
                m_yld[pp] = p.dma(f"d_yl{pp % 2}", [m_agy[grp], m_smm.get((pp - 2, TCD - 1))], [
                    (yp[:, sp * NYSP:(sp + 1) * NYSP, :], yv[:, sp * NYSP:(sp + 1) * NYSP, gl:gl + c.DP])
                    for sp in range(c.NDC // NYSP)])
                for tc in range(TCD):
                    k = pp * TCD + tc
                    row0 = pp * c.DP + tc * 128
                    m_sld2[(pp, tc)] = p.dma(f"d_sl2{(pp * TCD + tc) % 2}", [m_cmb.get((pp - 2, tc))], [
                        (gtl[k % 4][:], g_dram.ap()[row0:row0 + 128, :]),
                        (xtl[k % 4][:], x_hs.ap()[row0:row0 + 128, :])])

                    def smm(t, pp=pp, tc=tc, k=k):
                        for dc in range(c.NDC):
                            mm = t.matmul(pss[k % 2][:],
                                          ypan[pp % 2][:, dc * c.DP + tc * 128:dc * c.DP + (tc + 1) * 128],
                                          wob[:, dc * c.HSL:(dc + 1) * c.HSL],
                                          start=(dc == 0), stop=(dc == c.NDC - 1))
                        return mm
                    m_smm[(pp, tc)] = p.on("tensor", [m_wo, m_yld[pp],
                                                      m_cmb.get((pp, tc - 2) if tc >= 2 else (pp - 1, TCD + tc - 2))], smm)

                    def cmb(v, k=k):
                        v.tensor_tensor(sgt[:], pss[k % 2][:], gtl[k % 4][:], OP.mult)
                        v.tensor_copy(jnk[:, 0:1], cst[:, 3:4])
                        return v.tensor_tensor(otl[k % 2][:], sgt[:], xtl[k % 4][:], OP.add)
                    m_cmb[(pp, tc)] = p.on("vector", [m_smm[(pp, tc)], m_sld2[(pp, tc)], m_d1_done,
                                                      m_ost.get((pp, tc - 2) if tc >= 2 else (pp - 1, TCD + tc - 2))], cmb)
                    m_ost[(pp, tc)] = p.dma(f"d_os{(pp * TCD + tc) % 2}", [m_cmb[(pp, tc)]], [(
                        out_hs.ap()[row0:row0 + 128, :], otl[k % 2][:])])

            if debug_outputs:
                dbg_xc = nc.dram_tensor("dbg_xc", [c.DSL, c.NTOK], F32, kind="ExternalOutput")
                dbg_gz = nc.dram_tensor("dbg_gz", [c.DSL, c.NTOK], F32, kind="ExternalOutput")
                dbg_sp = nc.dram_tensor("dbg_sp", [c.E, c.NTOK], F32, kind="ExternalOutput")
                dbg_y = nc.dram_tensor("dbg_y", [c.DSL, c.NTOK], BF16, kind="ExternalOutput")
                dbg_g = nc.dram_tensor("dbg_g", [c.NTOK, c.HSL], F32, kind="ExternalOutput")
                dbg_xnt = nc.dram_tensor("dbg_xnt", [c.H, c.TS_TOK], BF16, kind="ExternalOutput")
                tws = [(dbg_xc.ap(), xc_dram.ap()), (dbg_gz.ap(), gz_dram.ap()),
                       (dbg_g.ap(), g_dram.ap()), (dbg_xnt.ap(), xnt_self.ap())]
                for g_ in range(c.NGRP):
                    tws.append((dbg_sp.ap()[:, g_ * c.GRP:(g_ + 1) * c.GRP], spR[g_].ap()))
                    tws.append((dbg_y.ap()[:, g_ * c.GRP:(g_ + 1) * c.GRP], y_self[g_].ap()))
                allm = [p.mark(k) for k in list(p.S)]
                p.dma("dbg", allm, tws)

            @block.sync
            def _(sy):
                for snm in list(p.S):
                    if snm not in ("v", "s", "t", "g") and p.cnt[snm] > 0:
                        sy.wait_ge(p.S[snm], p.cnt[snm])

    return nc



def make_inmaps(cfg, inputs):
    c = cfg
    x = np.asarray(inputs["L0_attn_hidden_states"], np.float32).reshape(c.NTOK, c.H)
    W1 = np.asarray(inputs["L0_attn_in_proj_w"], np.float32)
    conv_w = np.asarray(inputs["L0_attn_conv_w"], np.float32)
    conv_b = np.asarray(inputs["L0_attn_conv_b"], np.float32)
    xpw = np.asarray(inputs["L0_attn_x_proj_w"], np.float32)
    dtw = np.asarray(inputs["L0_attn_dt_proj_w"], np.float32)
    dt_bias = np.asarray(inputs["L0_attn_dt_bias"], np.float32)
    A_log = np.asarray(inputs["L0_attn_A_log"], np.float32)
    D_par = np.asarray(inputs["L0_attn_D_param"], np.float32)
    Wout = np.asarray(inputs["L0_attn_out_proj_w"], np.float32)
    dt_ln = np.asarray(inputs["L0_attn_dt_ln_w"], np.float32)
    B_ln = np.asarray(inputs["L0_attn_B_ln_w"], np.float32)
    C_ln = np.asarray(inputs["L0_attn_C_ln_w"], np.float32)
    ident = np.asarray(inputs["L0_ffn_identity"], np.float32)

    NN = c.N
    selBC = np.zeros((2 * NN, 2 * NN, 128), np.float32)
    for k in range(2 * NN):
        selBC[k, k, :] = 1.0
    selBC = selBC.reshape(2 * NN, 2 * NN * 128)
    eye = np.eye(128, dtype=ml_dtypes.bfloat16)
    ones_col = np.ones((128, 1), np.float32)
    selB_col = np.zeros((2 * NN, 1), np.float32); selB_col[:NN] = 1.0
    selC_col = np.zeros((2 * NN, 1), np.float32); selC_col[NN:] = 1.0
    sel_bc_rows = np.zeros((1, 64), np.float32)
    sel_bc_rows[0, :NN] = 1.0
    sel_bc_rows[0, 32 + NN:64] = 1.0
    ones_1x128 = np.ones((1, 128), np.float32)

    maps = []
    for cc in range(NCORES):
        dsl = slice(cc * c.DSL, (cc + 1) * c.DSL)
        hsl = slice(cc * c.HSL, (cc + 1) * c.HSL)
        maps.append(dict(
            x_tok=np.ascontiguousarray(x[cc * c.TS_TOK:(cc + 1) * c.TS_TOK]),
            x_hs=np.ascontiguousarray(x[:, hsl]),
            w1x=np.ascontiguousarray(W1[:, dsl]),
            w1z=np.ascontiguousarray(W1[:, c.DIN + cc * c.DSL: c.DIN + (cc + 1) * c.DSL]),
            convw=np.ascontiguousarray(conv_w[dsl]),
            convb=np.ascontiguousarray(conv_b[dsl][:, None]),
            xpwT=np.ascontiguousarray(xpw.T[dsl]),
            dtwT=np.ascontiguousarray(dtw.T[:, dsl]),
            dtbias=np.ascontiguousarray(dt_bias[dsl][:, None]),
            alog=np.ascontiguousarray(A_log[dsl]),
            dpar=np.ascontiguousarray(D_par[dsl][:, None]),
            dtln=np.ascontiguousarray(dt_ln[:, None]),
            wBC=np.ascontiguousarray(np.concatenate([B_ln, C_ln])[:, None]),
            woutT=np.ascontiguousarray(Wout.T[:, hsl]),
            iT=np.ascontiguousarray(ident.T[:, hsl]),
            eye128=eye,
            selBC=selBC,
            ones_col=ones_col,
            selB_col=selB_col,
            selC_col=selC_col,
            sel_bc_rows=sel_bc_rows,
            ones_1x128=ones_1x128,
        ))
    return maps


def kernel(**inputs):
    cfg = Cfg()
    nc = build_program(cfg)
    maps = make_inmaps(cfg, inputs)
    res = run_bass_kernel_spmd(nc, maps, core_ids=list(range(NCORES)))
    out = np.concatenate([np.asarray(r["out_hs"], np.float32) for r in res.results], axis=1)
    return out.reshape(cfg.B, cfg.L, cfg.H)


# version 07
# speedup vs baseline: 1.0372x; 1.0372x over previous
import numpy as np
import ml_dtypes
from contextlib import ExitStack
from dataclasses import dataclass

import concourse.bass as bass
import concourse.mybir as mybir
from concourse.bass_utils import run_bass_kernel_spmd

F32 = mybir.dt.float32
BF16 = mybir.dt.bfloat16
AF = mybir.ActivationFunctionType
OP = mybir.AluOpType
NCORES = 8


@dataclass
class Cfg:
    H: int = 4096
    DIN: int = 8192
    L: int = 8192
    B: int = 2
    R: int = 256
    N: int = 16
    K: int = 4
    TPB: int = 512
    TCC: int = 1024
    JH: int = 4
    DP: int = 256
    GRP: int = 2048
    eps_ln: float = 1e-5
    eps_rms: float = 1e-6

    @property
    def NTOK(self): return self.B * self.L
    @property
    def TS_TOK(self): return self.NTOK // NCORES
    @property
    def DSL(self): return self.DIN // NCORES
    @property
    def HSL(self): return self.H // NCORES
    @property
    def E(self): return self.R + 2 * self.N
    @property
    def NCB(self): return self.DSL // 128
    @property
    def NHC(self): return self.H // 128
    @property
    def NPAN(self): return self.NTOK // self.TPB
    @property
    def NCHK(self): return self.NTOK // self.TCC
    @property
    def NGRP(self): return self.NTOK // self.GRP
    @property
    def RCH(self): return self.R // 128
    @property
    def NDC(self): return self.DIN // 128

    def subs(self, w):
        out, o = [], 0
        while o < w:
            out.append((o, min(512, w - o)))
            o += 512
        return out


class P:
    ENG_SEM = {"vector": "v", "scalar": "s", "tensor": "t", "gpsimd": "g"}

    def __init__(self, nc, block, mksem):
        self.nc, self.block = nc, block
        self.mksem = mksem
        self.S = {}
        self.cnt = {}
        self.last_wait = {e: {} for e in ["vector", "scalar", "tensor", "gpsimd", "sync"]}
        self.fence_marks = []
        self.fenced = set()
        for k in ("v", "s", "t", "g", "cc"):
            self.sem(k)

    def sem(self, name):
        if name not in self.S:
            self.S[name] = self.mksem(name)
            self.cnt[name] = 0
        return self.S[name]

    def mark(self, sem):
        return (sem, self.cnt[sem])

    def new_phase(self):
        self.fence_marks = [(k, self.cnt[k]) for k in self.S]
        self.fenced = set()

    def _waits(self, eng, eng_name, waits):
        if eng_name not in self.fenced:
            self.fenced.add(eng_name)
            waits = list(self.fence_marks) + list(waits)
        own = self.ENG_SEM.get(eng_name)
        best = {}
        for wm in waits:
            if wm is None:
                continue
            snm, v = wm
            if v <= 0 or snm == own:
                continue
            if v > best.get(snm, 0):
                best[snm] = v
        lw = self.last_wait[eng_name]
        for snm in sorted(best):
            v = best[snm]
            if v > lw.get(snm, 0):
                eng.wait_ge(self.S[snm], v)
                lw[snm] = v

    def on(self, eng_name, waits, body):
        sem = self.ENG_SEM[eng_name]
        me = self

        @getattr(self.block, eng_name)
        def _(eng):
            me._waits(eng, eng_name, waits)
            last = body(eng)
            assert last is not None
            me.cnt[sem] += 1
            if eng_name in ("vector", "scalar", "gpsimd"):
                eng.drain().then_inc(me.S[sem], 1)
            else:
                last.then_inc(me.S[sem], 1)
        return self.mark(sem)

    def dma(self, semname, waits, transfers, queue="sy", slow=False):
        eng_name = "sync" if queue == "sy" else "gpsimd"
        self.sem(semname)
        me = self

        @getattr(self.block, eng_name)
        def _(eng):
            me._waits(eng, eng_name, waits)
            for (o, i) in transfers:
                me.cnt[semname] += 16
                eng.dma_start(out=o, in_=i,
                              allow_slow_non_contiguous=slow).then_inc(me.S[semname], 16)
        return self.mark(semname)

    def cc(self, waits, kind, op, ins, outs):
        me = self

        @self.block.gpsimd
        def _(eng):
            me._waits(eng, "gpsimd", waits)
            me.cnt["cc"] += 1
            eng.collective_compute(kind, op, replica_groups=[list(range(NCORES))],
                                   ins=ins, outs=outs).then_inc(me.S["cc"], 1)
        return self.mark("cc")


def build_program(cfg: Cfg, debug_outputs=False, phases="ABCD"):
    c = cfg
    assert c.TS_TOK % 128 == 0 and c.DSL % 128 == 0 and c.H % 128 == 0
    assert c.R % 128 == 0 and c.N == 16 and c.K == 4
    assert c.L % c.TCC == 0 and c.L % c.TPB == 0
    assert c.GRP % c.TPB == 0 and c.GRP % c.TCC == 0 and c.NTOK % c.GRP == 0
    assert c.GRP % c.DP == 0 and c.NCB % c.JH == 0 and c.DP % 128 == 0

    nc = bass.Bass("TRN2", target_bir_lowering=False, detect_race_conditions=False)

    def din(name, shape, dt=F32):
        return nc.dram_tensor(name, shape, dt, kind="ExternalInput")

    x_tok = din("x_tok", [c.TS_TOK, c.H])
    x_hs = din("x_hs", [c.NTOK, c.HSL])
    w1x = din("w1x", [c.H, c.DSL])
    w1z = din("w1z", [c.H, c.DSL])
    convw = din("convw", [c.DSL, c.K])
    convb = din("convb", [c.DSL, 1])
    xpwT = din("xpwT", [c.DSL, c.E])
    dtwT = din("dtwT", [c.R, c.DSL])
    dtbias = din("dtbias", [c.DSL, 1])
    alog = din("alog", [c.DSL, c.N])
    dpar = din("dpar", [c.DSL, 1])
    dtln = din("dtln", [c.R, 1])
    wBC = din("wBC", [2 * c.N, 1])
    woutT = din("woutT", [c.DIN, c.HSL])
    iT = din("iT", [c.H, c.HSL])
    eye128 = din("eye128", [128, 128], BF16)
    selBC = din("selBC", [2 * c.N, 2 * c.N * 128])
    ones_col = din("ones_col", [128, 1])
    selB_col = din("selB_col", [2 * c.N, 1])
    selC_col = din("selC_col", [2 * c.N, 1])
    sel_bc_rows = din("sel_bc_rows", [1, 64])
    ones_1x128 = din("ones_1x128", [1, 128])

    out_hs = nc.dram_tensor("out_hs", [c.NTOK, c.HSL], F32, kind="ExternalOutput")

    xnt_self = nc.dram_tensor("xnt_self", [c.H, c.TS_TOK], BF16)
    xnt_all = nc.dram_tensor("xnt_all", [NCORES * c.H, c.TS_TOK], BF16)
    spP = [nc.dram_tensor(f"spP{g}", [c.E, c.GRP], F32) for g in range(c.NGRP)]
    spR = [nc.dram_tensor(f"spR{g}", [c.E, c.GRP], F32) for g in range(c.NGRP)]
    xc_dram = nc.dram_tensor("xc_dram", [c.DSL, c.NTOK], F32)
    gz_dram = nc.dram_tensor("gz_dram", [c.DSL, c.NTOK], F32)
    y_self = [nc.dram_tensor(f"y_self{g}", [c.DSL, c.GRP], BF16) for g in range(c.NGRP)]
    y_all = [nc.dram_tensor(f"y_all{g}", [NCORES * c.DSL, c.GRP], BF16) for g in range(c.NGRP)]
    g_dram = nc.dram_tensor("g_dram", [c.NTOK, c.HSL], F32)

    with ExitStack() as ctx:
        E_ = ctx.enter_context
        cst = E_(nc.sbuf_tensor("consts", [128, 4], F32))
        jnk = E_(nc.sbuf_tensor("jnk", [128, 4], F32))
        block = E_(nc.Block())
        p = P(nc, block, lambda name: E_(nc.semaphore(name)))

        def setconsts(v):
            v.memset(cst[:, 0:1], c.eps_ln)
            v.memset(cst[:, 1:2], c.eps_rms)
            v.memset(cst[:, 3:4], 0.0)
            return v.memset(cst[:, 2:3], 1.0)
        m_cst = p.on("vector", [], setconsts)
        nc.const_aps.register(F32, c.eps_ln, cst[:, 0:1])
        nc.const_aps.register(F32, c.eps_rms, cst[:, 1:2])
        nc.const_aps.register(F32, 1.0, cst[:, 2:3])

        NT_A = c.TS_TOK // 128
        with ExitStack() as actx:
            EA = actx.enter_context
            xt = [EA(nc.sbuf_tensor(f"a_xt{i}", [128, c.H], F32)) for i in range(2)]
            xnb = [EA(nc.sbuf_tensor(f"a_xnb{i}", [128, c.H], BF16)) for i in range(2)]
            sqd = EA(nc.sbuf_tensor("a_sqd", [128, c.H], F32))
            stt = EA(nc.sbuf_tensor("a_stat", [128, 8], F32))
            ident = EA(nc.sbuf_tensor("a_ident", [128, 128], BF16))
            tpe = [EA(nc.sbuf_tensor(f"a_tpe{i}", [128, 128], BF16)) for i in range(2)]
            stt_hist = EA(nc.sbuf_tensor("a_stt_hist", [128, 8 * NT_A], F32))
            pst = [EA(nc.psum_tensor(f"a_pst{i}", [128, 128], BF16)) for i in range(2)]

            m_id = p.dma("a_id", [], [(ident[:], eye128.ap())])
            m_ld, m_xn, m_tp, m_ev, m_st = {}, {}, {}, {}, {}
            for i in range(min(2, NT_A)):
                m_ld[i] = p.dma(f"a_ld{i % 2}", [],
                                [(xt[i % 2][:], x_tok.ap()[i * 128:(i + 1) * 128, :])])

            for i in range(NT_A):
                def stats(v, i=i):
                    v.tensor_tensor(sqd[:], xt[i % 2][:], xt[i % 2][:], OP.mult)
                    v.tensor_reduce(stt[:, 0:1], xt[i % 2][:],
                                    axis=mybir.AxisListType.X, op=OP.add)
                    v.tensor_reduce(stt[:, 1:2], sqd[:], axis=mybir.AxisListType.X, op=OP.add)
                    v.tensor_scalar(stt[:, 2:3], stt[:, 0:1], 1.0 / c.H, 0.0, OP.mult, OP.add)
                    v.tensor_scalar(stt[:, 3:4], stt[:, 1:2], 1.0 / c.H, 0.0, OP.mult, OP.add)
                    v.tensor_tensor(stt[:, 4:5], stt[:, 2:3], stt[:, 2:3], OP.mult)
                    v.tensor_copy(jnk[:, 0:1], cst[:, 3:4])
                    return v.tensor_tensor(stt[:, 5:6], stt[:, 3:4], stt[:, 4:5], OP.subtract)
                m_va = p.on("vector", [m_ld[i]], stats)

                def rstd(sc):
                    sc.activation(stt[:, 6:7], stt[:, 5:6], AF.Ln, bias=c.eps_ln)
                    sc.copy(jnk[:, 1:2], cst[:, 3:4])
                    return sc.activation(stt[:, 7:8], stt[:, 6:7], AF.Exp, scale=-0.5)
                m_rs = p.on("scalar", [m_va, m_cst, m_xn.get(i - 1)], rstd)


                w = [m_rs, m_va, m_tp.get((i - 2, c.NHC - 1))]
                m_xn[i] = p.on("vector", w, lambda v, i=i: (
                    v.tensor_scalar(xnb[i % 2][:], xt[i % 2][:], stt[:, 2:3], stt[:, 7:8],
                                    OP.subtract, OP.mult)))
                if i + 2 < NT_A:
                    m_ld[i + 2] = p.dma(f"a_ld{i % 2}", [m_xn[i]], [(
                        xt[i % 2][:], x_tok.ap()[(i + 2) * 128:(i + 3) * 128, :])])

                for hc in range(c.NHC):
                    k = i * c.NHC + hc
                    m_tp[(i, hc)] = p.on("tensor", [m_xn[i], m_id, m_ev.get(k - 2)],
                                         lambda t, i=i, hc=hc, k=k: (
                        t.transpose(pst[k % 2][:], xnb[i % 2][:, hc * 128:(hc + 1) * 128], ident[:])))
                    m_ev[k] = p.on("scalar", [m_tp[(i, hc)], m_st.get(k - 2)],
                                   lambda sc, k=k: sc.copy(tpe[k % 2][:], pst[k % 2][:]))
                    m_st[k] = p.dma(f"a_st{k % 2}", [m_ev[k]], [(
                        xnt_self.ap()[hc * 128:(hc + 1) * 128, i * 128:(i + 1) * 128],
                        tpe[k % 2][:])])

            m_agx = p.cc([m_st[NT_A * c.NHC - 1]], "AllGather", OP.bypass,
                         [xnt_self.ap()], [xnt_all.ap()])

        if "B" not in phases:
            if debug_outputs:
                dbg_xnt = nc.dram_tensor("dbg_xnt", [c.H, c.TS_TOK], BF16, kind="ExternalOutput")
                dbg_stt = nc.dram_tensor("dbg_stt", [128, 8 * (c.TS_TOK // 128)], F32, kind="ExternalOutput")
                allm = [p.mark(k) for k in list(p.S)]
                p.dma("dbg", allm, [(dbg_xnt.ap(), xnt_self.ap()), (dbg_stt.ap(), stt_hist[:])])

            @block.sync
            def _(sy):
                for snm in list(p.S):
                    if snm not in ("v", "s", "t", "g") and p.cnt[snm] > 0:
                        sy.wait_ge(p.S[snm], p.cnt[snm])
            return nc

        p.new_phase()
        PANPG = c.GRP // c.TPB
        EBL = []
        o = 0
        while o < c.E:
            EBL.append((o, min(128, c.E - o)))
            o += 128
        with ExitStack() as bctx:
            EB = bctx.enter_context
            wsb = EB(nc.sbuf_tensor("b_w", [128, c.NHC * c.DSL], BF16))
            xpw = EB(nc.sbuf_tensor("b_xpw", [128, c.NCB * c.E], BF16))
            xk = [[EB(nc.sbuf_tensor(f"b_xk{bu}_{hc}", [128, c.TPB], BF16))
                   for hc in range(c.NHC)] for bu in range(2)]
            xcv = [EB(nc.sbuf_tensor(f"b_xcv{cb}", [128, c.TPB + 3], F32)) for cb in range(c.NCB)]
            cpre = EB(nc.sbuf_tensor("b_cpre", [128, c.TPB], F32))
            xcf = [EB(nc.sbuf_tensor(f"b_xcf{i}", [128, c.TPB], F32)) for i in range(2)]
            xcb = [EB(nc.sbuf_tensor(f"b_xcb{i}", [128, c.TPB], BF16)) for i in range(2)]
            cwt = EB(nc.sbuf_tensor("b_cw", [128, c.NCB * c.K], F32))
            cbt = EB(nc.sbuf_tensor("b_cb", [128, c.NCB], F32))
            spt = [EB(nc.sbuf_tensor(f"b_sp{i}", [ew, c.TPB], F32)) for i, (eo, ew) in enumerate(EBL)]
            gzt = [EB(nc.sbuf_tensor(f"b_gz{i}", [128, c.TPB], F32)) for i in range(2)]
            psx = [EB(nc.psum_tensor(f"b_psx{i}", [128, c.TPB], F32)) for i in range(4)]
            pse = [EB(nc.psum_tensor(f"b_pse{i}", [ew, c.TPB], F32)) for i, (eo, ew) in enumerate(EBL)]

            def wblk(hc, cb):
                q = hc * c.NCB + cb
                return wsb[:, q * 128:(q + 1) * 128]

            m_wx = p.dma("b_w", [], [(wblk(hc, cb),
                                      w1x.ap()[hc * 128:(hc + 1) * 128, cb * 128:(cb + 1) * 128])
                                     for hc in range(c.NHC) for cb in range(c.NCB)], queue="gd")
            tw = [(xpw[:, cb * c.E:(cb + 1) * c.E], xpwT.ap()[cb * 128:(cb + 1) * 128, :])
                  for cb in range(c.NCB)]
            tw.append((cwt[:].rearrange("q (cb k) -> q cb k", k=c.K),
                       convw.ap().rearrange("(cb q) k -> q cb k", q=128)))
            tw.append((cbt[:].unsqueeze(2), convb.ap().rearrange("(cb q) o -> q cb o", q=128)))
            m_waux = p.dma("b_waux", [], tw, queue="gd", slow=True)

            def rank_of(tok):
                return tok // c.TS_TOK, tok % c.TS_TOK

            m_xk = {}
            m_mm = {}
            m_evac = {}
            m_conv = {}
            m_silu = {}
            m_cast = {}
            m_xpmm = {}
            m_spev = {}
            m_xcst = {}
            m_zev = {}
            m_zst = {}
            m_spst = {}
            m_ar = {}

            XPD = 2
            NGX = c.NPAN * c.NCB

            def emit_xpmm(g2):
                pp2, cb2 = g2 // c.NCB, g2 % c.NCB
                w = [m_cast[g2], m_waux]
                if cb2 == 0:
                    w.append(m_spev.get(pp2 - 1))

                def xpmm(t, cb=cb2, gidx=g2):
                    for bi, (eo, ew) in enumerate(EBL):
                        mm = t.matmul(pse[bi][0:ew, :], xpw[:, cb * c.E + eo:cb * c.E + eo + ew],
                                      xcb[gidx % 2][:], start=(cb == 0), stop=(cb == c.NCB - 1))
                    return mm
                m_xpmm[(pp2, cb2)] = p.on("tensor", w, xpmm)
                if cb2 == c.NCB - 1:
                    def spevac(sc, pp=pp2):
                        for bi, (eo, ew) in enumerate(EBL):
                            e = sc.copy(spt[bi][0:ew, :], pse[bi][0:ew, :])
                        return e
                    m_spev[pp2] = p.on("scalar", [m_xpmm[(pp2, c.NCB - 1)], m_spst.get(pp2 - 1)],
                                       spevac)
                    grp2, gl2 = (pp2 * c.TPB) // c.GRP, (pp2 * c.TPB) % c.GRP
                    m_spst[pp2] = p.dma("b_sps", [m_spev[pp2]], [
                        (spP[grp2].ap()[eo:eo + ew, gl2:gl2 + c.TPB], spt[bi][0:ew, :])
                        for bi, (eo, ew) in enumerate(EBL)])
                    if (pp2 + 1) % PANPG == 0:
                        m_ar[grp2] = p.cc([m_spst[pp2]], "AllReduce", OP.add,
                                          [spP[grp2].ap()], [spR[grp2].ap()])

            for pp in range(c.NPAN):
                rk, tl = rank_of(pp * c.TPB)
                m_xk[(0, pp)] = p.dma(f"b_xk{pp % 2}", [m_agx, m_mm.get((0, pp - 2, c.NCB - 1))], [
                    (xk[pp % 2][hc][:],
                     xnt_all.ap()[rk * c.H + hc * 128: rk * c.H + (hc + 1) * 128, tl:tl + c.TPB])
                    for hc in range(c.NHC)])

                for cb in range(c.NCB):
                    gidx = pp * c.NCB + cb
                    bank = gidx % 4

                    def mmx(t, cb=cb, pp=pp, bank=bank):
                        for hc in range(c.NHC):
                            mm = t.matmul(psx[bank][:], wblk(hc, cb), xk[pp % 2][hc][:],
                                          start=(hc == 0), stop=(hc == c.NHC - 1))
                        return mm
                    m_mm[(0, pp, cb)] = p.on(
                        "tensor", [m_wx, m_xk[(0, pp)], m_evac.get(gidx - 4)], mmx)

                    if gidx >= XPD:
                        emit_xpmm(gidx - XPD)

                    m_evac[gidx] = p.on(
                        "scalar", [m_mm[(0, pp, cb)], m_conv.get((pp - 1, cb))],
                        lambda sc, cb=cb, bank=bank: sc.copy(xcv[cb][:, 3:3 + c.TPB], psx[bank][:]))

                    seq_start = (pp * c.TPB) % c.L == 0

                    def conv(v, cb=cb, seq_start=seq_start):
                        vv = xcv[cb]
                        if seq_start:
                            v.memset(vv[:, 0:3], 0.0)
                        v.tensor_scalar(cpre[:], vv[:, 0:c.TPB], cwt[:, cb * c.K:cb * c.K + 1],
                                        cbt[:, cb:cb + 1], OP.mult, OP.add)
                        v.tensor_copy(jnk[:, 0:1], cst[:, 3:4])
                        for k in range(1, c.K):
                            v.scalar_tensor_tensor(cpre[:], vv[:, k:k + c.TPB],
                                                   cwt[:, cb * c.K + k:cb * c.K + k + 1],
                                                   cpre[:], OP.mult, OP.add)
                            if k < c.K - 1:
                                v.tensor_copy(jnk[:, 0:1], cst[:, 3:4])
                        return v.tensor_copy(vv[:, 0:3], vv[:, c.TPB:c.TPB + 3])
                    m_conv[(pp, cb)] = p.on("vector", [m_evac[gidx], m_waux], conv)

                    m_silu[gidx] = p.on(
                        "scalar", [m_conv[(pp, cb)], m_cast.get(gidx - 2), m_xcst.get(gidx - 2)],
                        lambda sc, gidx=gidx: sc.activation(xcf[gidx % 2][:], cpre[:], AF.Silu))

                    m_cast[gidx] = p.on(
                        "gpsimd", [m_silu[gidx], m_xpmm.get(((gidx - 2) // c.NCB, (gidx - 2) % c.NCB))],
                        lambda g, gidx=gidx: g.tensor_copy(xcb[gidx % 2][:], xcf[gidx % 2][:]))

                    m_xcst[gidx] = p.dma(f"b_xcs{gidx % 2}", [m_silu[gidx]], [(
                        xc_dram.ap()[cb * 128:(cb + 1) * 128, pp * c.TPB:(pp + 1) * c.TPB],
                        xcf[gidx % 2][:])])

            for g2 in range(NGX - XPD, NGX):
                emit_xpmm(g2)

            m_wz = p.dma("b_w", [m_mm[(0, c.NPAN - 1, c.NCB - 1)]],
                         [(wblk(hc, cb),
                           w1z.ap()[hc * 128:(hc + 1) * 128, cb * 128:(cb + 1) * 128])
                          for hc in range(c.NHC) for cb in range(c.NCB)], queue="gd")

            for pp in range(c.NPAN):
                rk, tl = rank_of(pp * c.TPB)
                wxk = [m_mm[(1, pp - 2, c.NCB - 1)]] if pp >= 2 else [m_mm[(0, c.NPAN - 1, c.NCB - 1)]]
                m_xk[(1, pp)] = p.dma(f"b_xk{pp % 2}", wxk,
                                      [(xk[pp % 2][hc][:],
                                        xnt_all.ap()[rk * c.H + hc * 128: rk * c.H + (hc + 1) * 128,
                                                     tl:tl + c.TPB]) for hc in range(c.NHC)])

                for cb in range(c.NCB):
                    zi = pp * c.NCB + cb
                    bank = zi % 4

                    def mmz(t, cb=cb, pp=pp, bank=bank):
                        for hc in range(c.NHC):
                            mm = t.matmul(psx[bank][:], wblk(hc, cb), xk[pp % 2][hc][:],
                                          start=(hc == 0), stop=(hc == c.NHC - 1))
                        return mm
                    wz = [m_wz, m_xk[(1, pp)]]
                    if zi >= 4:
                        wz.append(m_zev[zi - 4])
                    else:
                        wz.append(m_evac[c.NPAN * c.NCB - 4 + zi])
                    m_mm[(1, pp, cb)] = p.on("tensor", wz, mmz)

                    m_zev[zi] = p.on("scalar", [m_mm[(1, pp, cb)], m_zst.get(zi - 2)],
                                     lambda sc, bank=bank, zi=zi: (
                        sc.activation(gzt[zi % 2][:], psx[bank][:], AF.Silu)))
                    m_zst[zi] = p.dma(f"b_zs{zi % 2}", [m_zev[zi]], [(
                        gz_dram.ap()[cb * 128:(cb + 1) * 128, pp * c.TPB:(pp + 1) * c.TPB],
                        gzt[zi % 2][:])])

            m_b_done_t = p.mark("t")
            m_b_done_zst = m_zst[c.NPAN * c.NCB - 1]

        p.new_phase()
        NJH = c.NCB // c.JH
        CHKPG = c.GRP // c.TCC
        with ExitStack() as cctx:
            EC = cctx.enter_context
            dtw_bf = EC(nc.sbuf_tensor("c_dtw", [128, c.RCH * c.DSL], BF16))
            selw = EC(nc.sbuf_tensor("c_selw", [2 * c.N, 2 * c.N * 128], F32))
            atile = EC(nc.sbuf_tensor("c_a", [128, c.NCB * c.N], F32))
            dtbt = EC(nc.sbuf_tensor("c_dtb", [128, c.NCB], F32))
            dpt = EC(nc.sbuf_tensor("c_dpt", [128, c.NCB], F32))
            nrm_c = EC(nc.sbuf_tensor("c_nc", [128, 1], F32))
            nrm_s = EC(nc.sbuf_tensor("c_ns", [2 * c.N, 2], F32))
            nrm_r = EC(nc.sbuf_tensor("c_nr", [1, 64 + 128], F32))
            wbct = EC(nc.sbuf_tensor("c_wbc", [2 * c.N, 1], F32))
            dtlnt = EC(nc.sbuf_tensor("c_dtln", [128, c.RCH], F32))
            spq = [EC(nc.sbuf_tensor(f"c_spq{i}", [128, c.TCC], F32)) for i in range(2)]
            spq2 = EC(nc.sbuf_tensor("c_spq2", [2 * c.N, c.TCC], F32))
            scr1 = EC(nc.sbuf_tensor("c_scr1", [128, c.TCC], F32))
            rr = EC(nc.sbuf_tensor("c_rr", [1, 3 * c.TCC], F32))
            dtr_bf = EC(nc.sbuf_tensor("c_dtrbf", [128, c.RCH * c.TCC], BF16))
            bcs = EC(nc.sbuf_tensor("c_bcs", [2 * c.N, c.TCC], F32))
            dt_j = [EC(nc.sbuf_tensor(f"c_dt{j}", [128, c.TCC], F32)) for j in range(c.JH)]
            dtx_j = [EC(nc.sbuf_tensor(f"c_dtx{j}", [128, c.TCC], F32)) for j in range(c.JH)]
            yac_j = [EC(nc.sbuf_tensor(f"c_yac{j}", [128, c.TCC], F32)) for j in range(c.JH)]
            xc_r = [EC(nc.sbuf_tensor(f"c_xcr{i}", [128, c.TCC], F32)) for i in range(c.JH)]
            gz_r = [EC(nc.sbuf_tensor(f"c_gzr{i}", [128, c.TCC], F32)) for i in range(2)]
            cbc_ab = [EC(nc.sbuf_tensor(f"c_cbc{i}", [128, c.TCC], F32)) for i in range(2)]
            da_b = [EC(nc.sbuf_tensor(f"c_da{i}", [128, c.TCC], F32)) for i in range(2)]
            dbx_b = [EC(nc.sbuf_tensor(f"c_dbx{i}", [128, c.TCC], F32)) for i in range(2)]
            hc_b = [EC(nc.sbuf_tensor(f"c_hcb{i}", [128, c.TCC], F32)) for i in range(2)]
            h_all = EC(nc.sbuf_tensor("c_hall", [128, c.JH * c.TCC], F32))
            state = EC(nc.sbuf_tensor("c_state", [128, c.N * c.NCB], F32))
            u_b = EC(nc.sbuf_tensor("c_u", [128, c.TCC], F32))
            yg_b = [EC(nc.sbuf_tensor(f"c_yg{i}", [128, c.TCC], BF16)) for i in range(2)]
            psA = [EC(nc.psum_tensor(f"c_psA{i}", [128, c.TCC], F32)) for i in range(4)]

            tw = [(selw[:], selBC.ap()), (wbct[:], wBC.ap()),
                  *[(dtlnt[:, r:r + 1], dtln.ap()[r * 128:(r + 1) * 128, :]) for r in range(cfg.RCH)],
                  (nrm_c[:, 0:1], ones_col.ap()),
                  (nrm_s[:, 0:1], selB_col.ap()), (nrm_s[:, 1:2], selC_col.ap()),
                  (nrm_r[:, 0:64], sel_bc_rows.ap()), (nrm_r[:, 64:192], ones_1x128.ap()),
                  (atile[:].rearrange("q (cb n) -> q cb n", n=c.N),
                   alog.ap().rearrange("(cb q) n -> q cb n", q=128)),
                  (dtbt[:].unsqueeze(2), dtbias.ap().rearrange("(cb q) o -> q cb o", q=128)),
                  (dpt[:].unsqueeze(2), dpar.ap().rearrange("(cb q) o -> q cb o", q=128))]
            tw += [(spq[r][:, 0:c.DSL], dtwT.ap()[r * 128:(r + 1) * 128, :]) for r in range(c.RCH)]
            m_cld = p.dma("c_cld", [], tw, slow=True)

            m_aexp = p.on("scalar", [m_cld], lambda sc: sc.activation(atile[:], atile[:], AF.Exp))

            def prep_v(v):
                v.tensor_scalar(atile[:], atile[:], -1.0, 0.0, OP.mult, OP.add)
                v.tensor_scalar(selw[:], selw[:], wbct[:, 0:1], 0.0, OP.mult, OP.add)
                for r in range(c.RCH):
                    v.tensor_scalar(dtw_bf[:, r * c.DSL:(r + 1) * c.DSL], spq[r][:, 0:c.DSL],
                                    dtlnt[:, r:r + 1], 0.0, OP.mult, OP.add)
                return v.memset(state[:], 0.0)
            m_prep = p.on("vector", [m_cld, m_aexp], prep_v)

            m_sld = {}
            m_sq = {}
            m_ssum = {}
            m_rst = {}
            m_rbc = {}
            m_scl = {}
            m_dtmm = {}
            m_sftp = {}
            m_dtg = {}
            m_bc = {}
            m_bcev = {}
            m_da = {}
            m_dbx = {}
            m_tts = {}
            m_hc = {}
            m_yad = {}
            m_stsv = {}
            m_xcld = {}
            m_gzld = {}
            m_u = {}
            m_yg = {}
            m_ygst = {}
            m_agy = {}
            m_gat_v = {}

            PR = {k: [] for k in ("A0", "A1", "A2", "A3")}

            def scan_prev_idx(ss):
                if ss < 0:
                    return None
                jj2 = ss % c.JH; ss //= c.JH
                n2 = ss % c.N; ss //= c.N
                return (ss // NJH, ss % NJH, n2, jj2)

            for q in range(c.NCHK):
                grp, gl = (q * c.TCC) // c.GRP, (q * c.TCC) % c.GRP
                seq_start = (q * c.TCC) % c.L == 0

                w = [m_ar[grp], m_prep, m_scl.get(q - 1), m_dtg.get((q - 1, c.NCB - 1))]
                m_sld[q] = p.dma("c_sld", w, [
                    (spq[0][:], spR[grp].ap()[0:128, gl:gl + c.TCC]),
                    (spq[1][:], spR[grp].ap()[128:256, gl:gl + c.TCC]),
                    (spq2[:], spR[grp].ap()[256:256 + 2 * c.N, gl:gl + c.TCC])])

                for blk in range(3):
                    src_t = [spq[0], spq[1], spq2][blk]
                    pw = [128, 128, 2 * c.N][blk]
                    w = [m_sld[q], m_ssum.get((q, blk - 1)), m_ssum.get((q - 1, 2))]
                    m_sq[(q, blk)] = p.on("scalar", w, lambda sc, src_t=src_t, pw=pw: (
                        sc.activation(scr1[0:pw, :], src_t[:], AF.Square)))
                    w = [m_sq[(q, blk)]] + PR["A0"]

                    def ssum(t, blk=blk):
                        for (so, sw) in c.subs(c.TCC):
                            if blk == 0:
                                mm = t.matmul(psA[0][0:1, so:so + sw], nrm_c[:, 0:1],
                                              scr1[:, so:so + sw], start=True, stop=False)
                            elif blk == 1:
                                mm = t.matmul(psA[0][0:1, so:so + sw], nrm_c[:, 0:1],
                                              scr1[:, so:so + sw], start=False, stop=True)
                            else:
                                t.matmul(psA[0][32:33, so:so + sw], nrm_s[:, 0:1],
                                         scr1[0:2 * c.N, so:so + sw], start=True, stop=True)
                                mm = t.matmul(psA[0][64:65, so:so + sw], nrm_s[:, 1:2],
                                              scr1[0:2 * c.N, so:so + sw], start=True, stop=True)
                        return mm
                    m_ssum[(q, blk)] = p.on("tensor", w, ssum)
                PR["A0"] = []

                def rstd3(sc, q=q):
                    sc.activation(rr[:, 0:c.TCC], psA[0][0:1, :], AF.Ln,
                                  scale=1.0 / c.R, bias=c.eps_rms)
                    sc.activation(rr[:, c.TCC:2 * c.TCC], psA[0][32:33, :], AF.Ln,
                                  scale=1.0 / c.N, bias=c.eps_rms)
                    sc.activation(rr[:, 0:c.TCC], rr[:, 0:c.TCC], AF.Exp, scale=-0.5)
                    sc.activation(rr[:, 2 * c.TCC:3 * c.TCC], psA[0][64:65, :], AF.Ln,
                                  scale=1.0 / c.N, bias=c.eps_rms)
                    sc.activation(rr[:, c.TCC:2 * c.TCC], rr[:, c.TCC:2 * c.TCC], AF.Exp, scale=-0.5)
                    return sc.activation(rr[:, 2 * c.TCC:3 * c.TCC], rr[:, 2 * c.TCC:3 * c.TCC],
                                         AF.Exp, scale=-0.5)
                m_rst[q] = p.on("scalar", [m_ssum[(q, 0)], m_ssum[(q, 1)], m_ssum[(q, 2)]], rstd3)
                PR["A0"].append(m_rst[q])

                w = [m_rst[q]] + PR["A1"] + PR["A2"]

                def rbc(t, q=q):
                    for (so, sw) in c.subs(c.TCC):
                        t.matmul(psA[1][:, so:so + sw], nrm_r[:, 64:192],
                                 rr[:, so:so + sw], start=True, stop=True)
                        t.matmul(psA[2][0:2 * c.N, so:so + sw], nrm_r[:, 0:32],
                                 rr[:, c.TCC + so:c.TCC + so + sw], start=True, stop=False)
                        mm = t.matmul(psA[2][0:2 * c.N, so:so + sw], nrm_r[:, 32:64],
                                      rr[:, 2 * c.TCC + so:2 * c.TCC + so + sw],
                                      start=False, stop=True)
                    return mm
                m_rbc[q] = p.on("tensor", w, rbc)
                PR["A1"] = []; PR["A2"] = []

                def scl(v, q=q):
                    v.tensor_tensor(dtr_bf[:, 0:c.TCC], spq[0][:], psA[1][:], OP.mult)
                    if c.RCH > 1:
                        v.tensor_tensor(dtr_bf[:, c.TCC:2 * c.TCC], spq[1][:], psA[1][:], OP.mult)
                    return v.tensor_tensor(bcs[:], spq2[:], psA[2][0:2 * c.N, :], OP.mult)
                m_scl[q] = p.on("vector", [m_rbc[q], m_dtmm.get((q - 1, c.NCB - 1))], scl)
                PR["A1"].append(m_scl[q]); PR["A2"].append(m_scl[q])

                for hlf in range(NJH):
                    for jj in range(c.JH):
                        j = hlf * c.JH + jj
                        pdt_name = "A0" if jj % 2 == 0 else "A3"
                        pdt = psA[0] if jj % 2 == 0 else psA[3]
                        w = [m_scl[q], m_rst[q]] + PR[pdt_name]

                        def dtmm(t, jj=jj, j=j, pdt=pdt):
                            for (so, sw) in c.subs(c.TCC):
                                for r in range(c.RCH):
                                    mm = t.matmul(pdt[:, so:so + sw],
                                                  dtw_bf[:, r * c.DSL + j * 128:r * c.DSL + (j + 1) * 128],
                                                  dtr_bf[:, r * c.TCC + so:r * c.TCC + so + sw],
                                                  start=(r == 0), stop=(r == c.RCH - 1))
                            return mm
                        m_dtmm[(q, j)] = p.on("tensor", w, dtmm)
                        PR[pdt_name] = []

                        def sftp(sc, j=j, jj=jj, pdt=pdt):
                            sc.activation(spq[0][:], pdt[:], AF.Abs, bias=dtbt[:, j:j + 1])
                            sc.activation(spq[1][:], pdt[:], AF.Relu, bias=dtbt[:, j:j + 1])
                            sc.activation(spq[0][:], spq[0][:], AF.Exp, scale=-1.0)
                            sc.copy(jnk[:, 1:2], cst[:, 3:4])
                            return sc.activation(spq[0][:], spq[0][:], AF.Ln, bias=1.0)
                        w = [m_dtmm[(q, j)], m_scl[q], m_dtg.get((q, j - 1))]
                        m_sftp[(q, j)] = p.on("scalar", w, sftp)
                        PR[pdt_name].append(m_sftp[(q, j)])

                        pq, pj = (q, (hlf - 1) * c.JH + jj) if hlf >= 1 else (q - 1, (NJH - 1) * c.JH + jj)
                        m_xcld[(q, j)] = p.dma(f"c_xcl{jj}", [m_dtg.get((pq, pj)), m_u.get((pq, pj))],
                                               [(xc_r[jj][:],
                                                 xc_dram.ap()[j * 128:(j + 1) * 128,
                                                              q * c.TCC:(q + 1) * c.TCC])])

                        def dtg(g, j=j, jj=jj):
                            g.tensor_tensor(dt_j[jj][:], spq[0][:], spq[1][:], OP.add)
                            g.tensor_copy(jnk[:, 2:3], cst[:, 3:4])
                            return g.tensor_tensor(dtx_j[jj][:], dt_j[jj][:], xc_r[jj][:], OP.mult)
                        w = [m_sftp[(q, j)], m_xcld[(q, j)]]
                        if (q, hlf) > (0, 0):
                            pqh = (q, hlf - 1) if hlf >= 1 else (q - 1, NJH - 1)
                            w += [m_da.get((pqh[0], pqh[1], c.N - 1, jj)),
                                  m_tts.get((pqh[0], pqh[1], c.N - 1, jj))]
                        m_dtg[(q, j)] = p.on("gpsimd", w, dtg)

                    def key_of(ss2):
                        n2 = (ss2 // c.JH) % c.N
                        jj2 = ss2 % c.JH
                        return (q, hlf, n2, jj2)

                    def emit_vblock(n_c, jj_c, ss_c, tslot):
                        j_c = hlf * c.JH + jj_c
                        w = [m_bc[(q, hlf, n_c)], m_dtg[(q, j_c)]]
                        if tslot is not None:
                            (tn, tjj, tss) = tslot
                            w.append(m_da[(q, hlf, tn, tjj)])
                            w.append(m_hc.get((q, hlf, tn - 1, tjj)) if tn >= 1 else
                                     (m_hc.get((q, hlf - 1, c.N - 1, tjj)) if hlf >= 1 else
                                      (m_hc.get((q - 1, NJH - 1, c.N - 1, tjj)) if q >= 1 else None)))

                        def vb(v, jj=jj_c, ss=ss_c, psB=psB, tslot=tslot, seq_start=seq_start):
                            r = v.scalar_tensor_tensor(dbx_b[ss % 2][:], dtx_j[jj][:],
                                                       cst[:, 2:3], psB[:], OP.mult, OP.mult)
                            if tslot is not None:
                                (tn, tjj, tss) = tslot
                                tj = hlf * c.JH + tjj
                                init = 0.0 if seq_start else state[:, tn * c.NCB + tj:tn * c.NCB + tj + 1]
                                r = v.tensor_tensor_scan(
                                    h_all[:, tjj * c.TCC:(tjj + 1) * c.TCC],
                                    da_b[tss % 2][:], dbx_b[tss % 2][:], init, OP.mult, OP.add)
                            return r
                        mk = p.on("vector", w, vb)
                        PR[psB_name].append(mk)
                        if tslot is not None:
                            (tn, tjj, tss) = tslot
                            m_tts[(q, hlf, tn, tjj)] = mk
                            if tjj == c.JH - 1:
                                def stsv(v, hlf=hlf, n=tn):
                                    return v.tensor_copy(
                                        state[:, n * c.NCB + hlf * c.JH: n * c.NCB + hlf * c.JH + c.JH],
                                        h_all[:].rearrange("z (jj t) -> z jj t", jj=c.JH)[:, :, c.TCC - 1])
                                m_stsv[(q, hlf, tn)] = p.on("vector", [mk], stsv)

                    def emit_tts_only(tslot):
                        (tn, tjj, tss) = tslot
                        w = [m_da[(q, hlf, tn, tjj)]]
                        w.append(m_hc.get((q, hlf, tn - 1, tjj)) if tn >= 1 else
                                 (m_hc.get((q, hlf - 1, c.N - 1, tjj)) if hlf >= 1 else
                                  (m_hc.get((q - 1, NJH - 1, c.N - 1, tjj)) if q >= 1 else None)))

                        def vb(v, tn=tn, tjj=tjj, tss=tss, seq_start=seq_start):
                            tj = hlf * c.JH + tjj
                            init = 0.0 if seq_start else state[:, tn * c.NCB + tj:tn * c.NCB + tj + 1]
                            return v.tensor_tensor_scan(
                                h_all[:, tjj * c.TCC:(tjj + 1) * c.TCC],
                                da_b[tss % 2][:], dbx_b[tss % 2][:], init, OP.mult, OP.add)
                        mk = p.on("vector", w, vb)
                        m_tts[(q, hlf, tn, tjj)] = mk
                        if tjj == c.JH - 1:
                            def stsv(v, hlf=hlf, n=tn):
                                return v.tensor_copy(
                                    state[:, n * c.NCB + hlf * c.JH: n * c.NCB + hlf * c.JH + c.JH],
                                    h_all[:].rearrange("z (jj t) -> z jj t", jj=c.JH)[:, :, c.TCC - 1])
                            m_stsv[(q, hlf, tn)] = p.on("vector", [mk], stsv)

                    def emit_gblock(hslot, yslot):
                        w = []
                        if hslot is not None:
                            (hn, hjj, hss, hcbc) = hslot
                            w += [m_tts[(q, hlf, hn, hjj)], m_bcev[(q, hlf, hn)]]
                        if yslot is not None:
                            (yn, yjj, yss) = yslot
                            w.append(m_hc[(q, hlf, yn, yjj)])

                        def gb(g, hslot=hslot, yslot=yslot):
                            r = None
                            if hslot is not None:
                                (hn, hjj, hss, hcbc) = hslot
                                r = g.tensor_tensor(hc_b[hss % 2][:],
                                                    h_all[:, hjj * c.TCC:(hjj + 1) * c.TCC],
                                                    hcbc[:], OP.mult)
                            if yslot is not None:
                                (yn, yjj, yss) = yslot
                                if yn == 0:
                                    r = g.tensor_copy(yac_j[yjj][:], hc_b[yss % 2][:])
                                else:
                                    r = g.tensor_tensor(yac_j[yjj][:], yac_j[yjj][:],
                                                        hc_b[yss % 2][:], OP.add)
                            return r
                        mk = p.on("gpsimd", w, gb)
                        if hslot is not None:
                            (hn, hjj, hss, hcbc) = hslot
                            m_hc[(q, hlf, hn, hjj)] = mk
                        if yslot is not None:
                            (yn, yjj, yss) = yslot
                            m_yad[(q, hlf, yn, yjj)] = mk

                    pend_tts = None
                    pend_hc = None
                    pend_yad = None
                    for n in range(c.N):
                        m = (q * NJH + hlf) * c.N + n
                        psB_name = "A1" if m % 2 == 0 else "A0"
                        psC_name = "A2" if m % 2 == 0 else "A3"
                        psB = {"A0": psA[0], "A1": psA[1], "A2": psA[2], "A3": psA[3]}[psB_name]
                        psC = {"A0": psA[0], "A1": psA[1], "A2": psA[2], "A3": psA[3]}[psC_name]
                        cbc_t = cbc_ab[m % 2]

                        w = [m_scl[q], m_dtg[(q, hlf * c.JH + c.JH - 1)]] + PR[psB_name] + PR[psC_name]

                        def bc(t, n=n, psB=psB, psC=psC):
                            for (so, sw) in c.subs(c.TCC):
                                t.matmul(psB[:, so:so + sw], selw[:, n * 128:(n + 1) * 128],
                                         bcs[:, so:so + sw], start=True, stop=True)
                                mm = t.matmul(psC[:, so:so + sw],
                                              selw[:, (c.N + n) * 128:(c.N + n + 1) * 128],
                                              bcs[:, so:so + sw], start=True, stop=True)
                            return mm
                        m_bc[(q, hlf, n)] = p.on("tensor", w, bc)
                        PR[psB_name] = []; PR[psC_name] = []

                        w = [m_bc[(q, hlf, n)]]
                        pm2 = scan_prev_idx((m - 2) * c.JH + (c.JH - 1))
                        if pm2 is not None:
                            w.append(m_hc.get(pm2))
                        m_bcev[(q, hlf, n)] = p.on("scalar", w, lambda sc, cbc_t=cbc_t, psC=psC: (
                            sc.copy(cbc_t[:], psC[:])))
                        PR[psC_name].append(m_bcev[(q, hlf, n)])

                        for jj in range(c.JH):
                            j = hlf * c.JH + jj
                            ss = m * c.JH + jj
                            w = [m_dtg[(q, j)]]
                            km2 = scan_prev_idx(ss - 2)
                            if km2 is not None:
                                w.append(m_tts.get(km2))
                            m_da[(q, hlf, n, jj)] = p.on("scalar", w, lambda sc, jj=jj, j=j, n=n, ss=ss: (
                                sc.activation(da_b[ss % 2][:], dt_j[jj][:], AF.Exp,
                                              scale=atile[:, j * c.N + n:j * c.N + n + 1])))

                            emit_vblock(n, jj, ss, pend_tts)
                            if pend_hc is not None:
                                emit_gblock(pend_hc, pend_yad)
                                pend_yad = pend_hc[:3]
                            if pend_tts is not None:
                                pend_hc = pend_tts + (pend_cbc,)
                            pend_tts = (n, jj, ss)
                            pend_cbc = cbc_t

                    emit_tts_only(pend_tts)
                    emit_gblock(pend_hc, pend_yad)
                    emit_gblock(pend_tts + (pend_cbc,), pend_hc[:3])
                    emit_gblock(None, pend_tts)
                    pend_tts = None
                    pend_hc = None
                    pend_yad = None

                    for jj in range(c.JH):
                        j = hlf * c.JH + jj
                        w = [m_b_done_zst, m_yg.get((q, hlf, jj - 2))]
                        if jj < 2 and (q, hlf) > (0, 0):
                            pq, ph = (q, hlf - 1) if hlf >= 1 else (q - 1, NJH - 1)
                            w.append(m_yg.get((pq, ph, c.JH + jj - 2)))
                        m_gzld[(q, hlf, jj)] = p.dma(f"c_gzl{jj % 2}", w, [
                            (gz_r[jj % 2][:],
                             gz_dram.ap()[j * 128:(j + 1) * 128, q * c.TCC:(q + 1) * c.TCC])])

                        w = [m_yad[(q, hlf, c.N - 1, jj)]]
                        if jj >= 1:
                            w.append(m_yg.get((q, hlf, jj - 1)))
                        else:
                            pq2, ph2 = (q, hlf - 1) if hlf >= 1 else (q - 1, NJH - 1)
                            w.append(m_yg.get((pq2, ph2, c.JH - 1)))
                        m_u[(q, j)] = p.on("vector", w, lambda v, jj=jj, j=j: (
                            v.scalar_tensor_tensor(u_b[:], xc_r[jj][:], dpt[:, j:j + 1],
                                                   yac_j[jj][:], OP.mult, OP.add)))

                        w = [m_u[(q, j)], m_gzld[(q, hlf, jj)], m_ygst.get((q, hlf, jj - 2))]
                        m_yg[(q, hlf, jj)] = p.on("gpsimd", w, lambda g, jj=jj: (
                            g.tensor_tensor(yg_b[jj % 2][:], u_b[:], gz_r[jj % 2][:], OP.mult)))

                        m_ygst[(q, hlf, jj)] = p.dma(f"c_ygs{jj % 2}", [m_yg[(q, hlf, jj)]], [(
                            y_self[grp].ap()[j * 128:(j + 1) * 128, gl:gl + c.TCC],
                            yg_b[jj % 2][:])])

                if (q + 1) % CHKPG == 0:
                    m_agy[grp] = p.cc([m_ygst[(q, NJH - 1, c.JH - 1)]], "AllGather", OP.bypass,
                                      [y_self[grp].ap()], [y_all[grp].ap()])

        p.new_phase()
        ND = c.NTOK // c.DP
        TCD = c.DP // 128
        with ExitStack() as dctx:
            ED = dctx.enter_context
            itb = ED(nc.sbuf_tensor("d_it", [128, c.NHC * c.HSL], BF16))
            xkd = [[ED(nc.sbuf_tensor(f"d_xk{bu}_{hc}", [128, c.DP], BF16))
                    for hc in range(c.NHC)] for bu in range(2)]
            gtd = [ED(nc.sbuf_tensor(f"d_gt{i}", [128, c.HSL], F32)) for i in range(2)]
            psg = [ED(nc.psum_tensor(f"d_psg{i}", [128, c.HSL], F32)) for i in range(2)]

            m_it = p.dma("d_it", [], [(itb[:, hc * c.HSL:(hc + 1) * c.HSL],
                                       iT.ap()[hc * 128:(hc + 1) * 128, :]) for hc in range(c.NHC)],
                         queue="gd")
            m_xkd, m_gmm, m_gsl, m_gst = {}, {}, {}, {}
            for pp in range(ND):
                rk = (pp * c.DP) // c.TS_TOK
                tl = (pp * c.DP) % c.TS_TOK
                m_xkd[pp] = p.dma(f"d_xkd{pp % 2}", [m_agx, m_gmm.get((pp - 2, TCD - 1))], [
                    (xkd[pp % 2][hc][:],
                     xnt_all.ap()[rk * c.H + hc * 128: rk * c.H + (hc + 1) * 128, tl:tl + c.DP])
                    for hc in range(c.NHC)])
                for tc in range(TCD):
                    k = pp * TCD + tc

                    def gmm(t, pp=pp, tc=tc, k=k):
                        for hc in range(c.NHC):
                            mm = t.matmul(psg[k % 2][:], xkd[pp % 2][hc][:, tc * 128:(tc + 1) * 128],
                                          itb[:, hc * c.HSL:(hc + 1) * c.HSL],
                                          start=(hc == 0), stop=(hc == c.NHC - 1))
                        return mm
                    m_gmm[(pp, tc)] = p.on("tensor", [m_it, m_xkd[pp],
                                                      m_gsl.get((pp, tc - 2) if tc >= 2 else (pp - 1, TCD + tc - 2))], gmm)
                    m_gsl[(pp, tc)] = p.on("scalar", [m_gmm[(pp, tc)],
                                                      m_gst.get((pp, tc - 2) if tc >= 2 else (pp - 1, TCD + tc - 2))],
                                           lambda sc, k=k: sc.activation(gtd[k % 2][:], psg[k % 2][:], AF.Silu))
                    m_gst[(pp, tc)] = p.dma(f"d_gst{(pp * TCD + tc) % 2}", [m_gsl[(pp, tc)]], [(
                        g_dram.ap()[k * 128:(k + 1) * 128, :], gtd[k % 2][:])])
            m_d1_done = m_gst[(ND - 1, TCD - 1)]

        p.new_phase()
        with ExitStack() as dctx2:
            ED = dctx2.enter_context
            wob = ED(nc.sbuf_tensor("d_wo", [128, c.NDC * c.HSL], BF16))
            ypan = [ED(nc.sbuf_tensor(f"d_yp{i}", [128, c.NDC * c.DP], BF16)) for i in range(2)]
            gtl = [ED(nc.sbuf_tensor(f"d_gtl{i}", [128, c.HSL], F32)) for i in range(4)]
            xtl = [ED(nc.sbuf_tensor(f"d_xtl{i}", [128, c.HSL], F32)) for i in range(4)]
            sgt = ED(nc.sbuf_tensor("d_sg", [128, c.HSL], F32))
            otl = [ED(nc.sbuf_tensor(f"d_otl{i}", [128, c.HSL], F32)) for i in range(2)]
            pss = [ED(nc.psum_tensor(f"d_pss{i}", [128, c.HSL], F32)) for i in range(2)]

            m_wo = p.dma("d_wo", [], [(wob[:, dc * c.HSL:(dc + 1) * c.HSL],
                                       woutT.ap()[dc * 128:(dc + 1) * 128, :]) for dc in range(c.NDC)],
                         queue="gd")
            m_yld, m_sld2, m_smm, m_cmb, m_ost = {}, {}, {}, {}, {}
            NYSP = max(1, c.NDC // 8)
            for pp in range(ND):
                grp = (pp * c.DP) // c.GRP
                gl = (pp * c.DP) % c.GRP
                yv = y_all[grp].ap().rearrange("(dc q) t -> q dc t", q=128)
                yp = ypan[pp % 2][:].rearrange("q (dc t) -> q dc t", t=c.DP)
                m_yld[pp] = p.dma(f"d_yl{pp % 2}", [m_agy[grp], m_smm.get((pp - 2, TCD - 1))], [
                    (yp[:, sp * NYSP:(sp + 1) * NYSP, :], yv[:, sp * NYSP:(sp + 1) * NYSP, gl:gl + c.DP])
                    for sp in range(c.NDC // NYSP)])
                for tc in range(TCD):
                    k = pp * TCD + tc
                    row0 = pp * c.DP + tc * 128
                    m_sld2[(pp, tc)] = p.dma(f"d_sl2{(pp * TCD + tc) % 2}", [m_cmb.get((pp - 2, tc))], [
                        (gtl[k % 4][:], g_dram.ap()[row0:row0 + 128, :]),
                        (xtl[k % 4][:], x_hs.ap()[row0:row0 + 128, :])])

                    def smm(t, pp=pp, tc=tc, k=k):
                        for dc in range(c.NDC):
                            mm = t.matmul(pss[k % 2][:],
                                          ypan[pp % 2][:, dc * c.DP + tc * 128:dc * c.DP + (tc + 1) * 128],
                                          wob[:, dc * c.HSL:(dc + 1) * c.HSL],
                                          start=(dc == 0), stop=(dc == c.NDC - 1))
                        return mm
                    m_smm[(pp, tc)] = p.on("tensor", [m_wo, m_yld[pp],
                                                      m_cmb.get((pp, tc - 2) if tc >= 2 else (pp - 1, TCD + tc - 2))], smm)

                    def cmb(v, k=k):
                        v.tensor_tensor(sgt[:], pss[k % 2][:], gtl[k % 4][:], OP.mult)
                        v.tensor_copy(jnk[:, 0:1], cst[:, 3:4])
                        return v.tensor_tensor(otl[k % 2][:], sgt[:], xtl[k % 4][:], OP.add)
                    m_cmb[(pp, tc)] = p.on("vector", [m_smm[(pp, tc)], m_sld2[(pp, tc)], m_d1_done,
                                                      m_ost.get((pp, tc - 2) if tc >= 2 else (pp - 1, TCD + tc - 2))], cmb)
                    m_ost[(pp, tc)] = p.dma(f"d_os{(pp * TCD + tc) % 2}", [m_cmb[(pp, tc)]], [(
                        out_hs.ap()[row0:row0 + 128, :], otl[k % 2][:])])

            if debug_outputs:
                dbg_xc = nc.dram_tensor("dbg_xc", [c.DSL, c.NTOK], F32, kind="ExternalOutput")
                dbg_gz = nc.dram_tensor("dbg_gz", [c.DSL, c.NTOK], F32, kind="ExternalOutput")
                dbg_sp = nc.dram_tensor("dbg_sp", [c.E, c.NTOK], F32, kind="ExternalOutput")
                dbg_y = nc.dram_tensor("dbg_y", [c.DSL, c.NTOK], BF16, kind="ExternalOutput")
                dbg_g = nc.dram_tensor("dbg_g", [c.NTOK, c.HSL], F32, kind="ExternalOutput")
                dbg_xnt = nc.dram_tensor("dbg_xnt", [c.H, c.TS_TOK], BF16, kind="ExternalOutput")
                tws = [(dbg_xc.ap(), xc_dram.ap()), (dbg_gz.ap(), gz_dram.ap()),
                       (dbg_g.ap(), g_dram.ap()), (dbg_xnt.ap(), xnt_self.ap())]
                for g_ in range(c.NGRP):
                    tws.append((dbg_sp.ap()[:, g_ * c.GRP:(g_ + 1) * c.GRP], spR[g_].ap()))
                    tws.append((dbg_y.ap()[:, g_ * c.GRP:(g_ + 1) * c.GRP], y_self[g_].ap()))
                allm = [p.mark(k) for k in list(p.S)]
                p.dma("dbg", allm, tws)

            @block.sync
            def _(sy):
                for snm in list(p.S):
                    if snm not in ("v", "s", "t", "g") and p.cnt[snm] > 0:
                        sy.wait_ge(p.S[snm], p.cnt[snm])

    return nc



def make_inmaps(cfg, inputs):
    c = cfg
    x = np.asarray(inputs["L0_attn_hidden_states"], np.float32).reshape(c.NTOK, c.H)
    W1 = np.asarray(inputs["L0_attn_in_proj_w"], np.float32)
    conv_w = np.asarray(inputs["L0_attn_conv_w"], np.float32)
    conv_b = np.asarray(inputs["L0_attn_conv_b"], np.float32)
    xpw = np.asarray(inputs["L0_attn_x_proj_w"], np.float32)
    dtw = np.asarray(inputs["L0_attn_dt_proj_w"], np.float32)
    dt_bias = np.asarray(inputs["L0_attn_dt_bias"], np.float32)
    A_log = np.asarray(inputs["L0_attn_A_log"], np.float32)
    D_par = np.asarray(inputs["L0_attn_D_param"], np.float32)
    Wout = np.asarray(inputs["L0_attn_out_proj_w"], np.float32)
    dt_ln = np.asarray(inputs["L0_attn_dt_ln_w"], np.float32)
    B_ln = np.asarray(inputs["L0_attn_B_ln_w"], np.float32)
    C_ln = np.asarray(inputs["L0_attn_C_ln_w"], np.float32)
    ident = np.asarray(inputs["L0_ffn_identity"], np.float32)

    NN = c.N
    selBC = np.zeros((2 * NN, 2 * NN, 128), np.float32)
    for k in range(2 * NN):
        selBC[k, k, :] = 1.0
    selBC = selBC.reshape(2 * NN, 2 * NN * 128)
    eye = np.eye(128, dtype=ml_dtypes.bfloat16)
    ones_col = np.ones((128, 1), np.float32)
    selB_col = np.zeros((2 * NN, 1), np.float32); selB_col[:NN] = 1.0
    selC_col = np.zeros((2 * NN, 1), np.float32); selC_col[NN:] = 1.0
    sel_bc_rows = np.zeros((1, 64), np.float32)
    sel_bc_rows[0, :NN] = 1.0
    sel_bc_rows[0, 32 + NN:64] = 1.0
    ones_1x128 = np.ones((1, 128), np.float32)

    maps = []
    for cc in range(NCORES):
        dsl = slice(cc * c.DSL, (cc + 1) * c.DSL)
        hsl = slice(cc * c.HSL, (cc + 1) * c.HSL)
        maps.append(dict(
            x_tok=np.ascontiguousarray(x[cc * c.TS_TOK:(cc + 1) * c.TS_TOK]),
            x_hs=np.ascontiguousarray(x[:, hsl]),
            w1x=np.ascontiguousarray(W1[:, dsl]),
            w1z=np.ascontiguousarray(W1[:, c.DIN + cc * c.DSL: c.DIN + (cc + 1) * c.DSL]),
            convw=np.ascontiguousarray(conv_w[dsl]),
            convb=np.ascontiguousarray(conv_b[dsl][:, None]),
            xpwT=np.ascontiguousarray(xpw.T[dsl]),
            dtwT=np.ascontiguousarray(dtw.T[:, dsl]),
            dtbias=np.ascontiguousarray(dt_bias[dsl][:, None]),
            alog=np.ascontiguousarray(A_log[dsl]),
            dpar=np.ascontiguousarray(D_par[dsl][:, None]),
            dtln=np.ascontiguousarray(dt_ln[:, None]),
            wBC=np.ascontiguousarray(np.concatenate([B_ln, C_ln])[:, None]),
            woutT=np.ascontiguousarray(Wout.T[:, hsl]),
            iT=np.ascontiguousarray(ident.T[:, hsl]),
            eye128=eye,
            selBC=selBC,
            ones_col=ones_col,
            selB_col=selB_col,
            selC_col=selC_col,
            sel_bc_rows=sel_bc_rows,
            ones_1x128=ones_1x128,
        ))
    return maps


def kernel(**inputs):
    cfg = Cfg()
    nc = build_program(cfg)
    maps = make_inmaps(cfg, inputs)
    res = run_bass_kernel_spmd(nc, maps, core_ids=list(range(NCORES)))
    out = np.concatenate([np.asarray(r["out_hs"], np.float32) for r in res.results], axis=1)
    return out.reshape(cfg.B, cfg.L, cfg.H)
